# version 20
# baseline (speedup 1.0000x reference)
import math
import numpy as np
import concourse.bass as bass
import concourse.mybir as mybir
from concourse.bass_utils import run_bass_kernel_spmd

F32 = mybir.dt.float32
BF16 = mybir.dt.bfloat16
AF = mybir.ActivationFunctionType
ALU = mybir.AluOpType
AX = mybir.AxisListType

S = 2048
DM = 2048
H = 16
DH = 128
KSEL = 128
FF = 4096
NT = S // 128
NC_ = DM // 128
NF1 = FF // 128
SCALE = 1.0 / math.sqrt(DH)
EPS = 1e-5
NBIN = 48
QT = 4
QS = S // QT


class Plan:
    ENGS = ("T", "V", "S", "Q", "G")

    def __init__(self, nc):
        self.nc = nc
        self.items = {e: [] for e in self.ENGS}
        self.sem_handles = {}
        self.cnt = {}

    def sem(self, name):
        if name not in self.sem_handles:
            self.sem_handles[name] = self.nc.alloc_semaphore(name)
            self.cnt[name] = 0
        return self.sem_handles[name]

    def emit(self, eng, fn, deps=(), inc=None):
        tok = None
        if inc is not None:
            self.sem(inc[0])
            self.cnt[inc[0]] += inc[1]
            tok = (inc[0], self.cnt[inc[0]])
        self.items[eng].append((tuple(deps), fn, inc))
        return tok

    def chain(self, eng, fn, deps=()):
        cn = "c" + eng
        self.sem(cn)
        prev = self.cnt[cn]
        d = list(deps)
        if prev > 0:
            d.append((cn, prev))
        return self.emit(eng, fn, deps=d, inc=(cn, 1))

    def tok(self, eng):
        cn = "c" + eng
        self.sem(cn)
        return (cn, self.cnt[cn])

    def replay(self, eng_name, engine):
        watermark = {}
        for deps, fn, inc in self.items[eng_name]:
            for sname, val in deps:
                if val > 0 and watermark.get(sname, 0) < val:
                    engine.wait_ge(self.sem_handles[sname], val)
                    watermark[sname] = val
            instr = fn(engine)
            if inc is not None:
                instr.then_inc(self.sem_handles[inc[0]], inc[1])


def build_nc():
    nc = bass.Bass("TRN2", target_bir_lowering=False,
                   detect_race_conditions=False)

    def dram_in(name, shape):
        return nc.dram_tensor(name, list(shape), F32, kind="ExternalInput")

    xt_h = dram_in("xt", (NC_, 128, S))
    xtok_h = dram_in("xtok", (S, DM))
    wkT_h = dram_in("wkT", (NC_, 128, DM))
    wvT_h = dram_in("wvT", (NC_, 128, DM))
    wqL_h = dram_in("wqL", (H, 128, NC_, 128))
    woT_h = dram_in("woT", (NC_, 128, DM))
    sw1T_h = dram_in("sw1T", (DH, DH))
    sb1_h = dram_in("sb1c", (DH, 1))
    sw2_h = dram_in("sw2c", (DH, 1))
    sb2_h = dram_in("sb2r", (1, H))
    w1L_h = dram_in("w1L", (NF1, 128, NC_, 128))
    b1_h = dram_in("b1b", (128, NF1))
    w2L_h = dram_in("w2L", (NF1, 128, NF1, 128))
    b2_h = dram_in("b2b", (128, NF1))
    w3T_h = dram_in("w3T", (NF1, 128, DM))
    b3_h = dram_in("b3r", (1, DM))

    y_h = nc.dram_tensor("y", [S, DM], F32, kind="ExternalOutput")
    y1s_h = nc.dram_tensor("y1scr", [NT, 128, DM], F32, kind="Internal")
    w1scr = nc.dram_tensor("w1scr", [NF1, 128, NC_ * 128], BF16, kind="Internal")
    y1Tscr = nc.dram_tensor("y1Tscr", [QT, 128, NC_, 4, 128], BF16, kind="Internal")
    qTscr = nc.dram_tensor("qTscr", [H, 128, NT * 128], BF16, kind="Internal")
    w2scr = nc.dram_tensor("w2scr", [NF1, 128, NF1 * 128], BF16, kind="Internal")
    w3scr = nc.dram_tensor("w3scr", [4, 128, NF1, 512], BF16, kind="Internal")

    TOP = nc.sbuf_top
    S1 = TOP - 64 * 1024
    S2 = TOP - 128 * 1024
    S3 = S2 - 32 * 1024
    S4 = S3 - 40 * 1024

    def at(name, shape, dtype, off):
        return nc.alloc_sbuf_tensor_at(name, list(shape), dtype, offset=off)

    def sm(name, shape, dtype=F32):
        return nc.alloc_sbuf_tensor(name, list(shape), dtype)

    ident_f = sm("ident_f", [128, 128], F32)
    ident_b = sm("ident_b", [128, 128], BF16)
    jrow = sm("jrow", [128, 128], F32)
    ones_b = sm("ones_b", [1, 128], BF16)
    sw1_b = sm("sw1_b", [128, 128], BF16)
    sw2_b = sm("sw2_b", [128, 1], BF16)
    sb1_s = sm("sb1_s", [128, 1], F32)
    sb2_s = sm("sb2_s", [1, H], BF16)
    b1_s = sm("b1_s", [128, NF1], F32)
    b2_s = sm("b2_s", [128, NF1], F32)
    zcol = sm("zcol", [128, 1], F32)
    eps_c = sm("eps_c", [128, 1], F32)
    m_c = sm("m_c", [128, 1], F32)
    nm_c = sm("nm_c", [128, 1], F32)
    se_c = sm("se_c", [128, 1], F32)
    rc_c = sm("rc_c", [128, 1], F32)
    mean_c = sm("mean_c", [128, 2], F32)
    sq_c = sm("sq_c", [128, 2], F32)
    var_c = sm("var_c", [128, 2], F32)
    sd_c = sm("sd_c", [128, 2], F32)
    rs_c = sm("rs_c", [128, 2], F32)
    lo_c = sm("lo_c", [H, 1], F32)
    hi_c = sm("hi_c", [H, 1], F32)
    mid_c = sm("mid_c", [H, 1], F32)
    mids_c = sm("mids_c", [H, 1], F32)
    cnt_c = sm("cnt_c", [H, 1], F32)
    sel_c = sm("sel_c", [H, 1], F32)
    d0_c = sm("d0_c", [H, 1], F32)
    mx4 = sm("mx4", [128, 2, 4], F32)
    nm4 = sm("nm4", [128, 2, 4], F32)
    se4 = sm("se4", [128, 2, 4], F32)
    rc4 = sm("rc4", [128, 2, 4], F32)
    kvt_sb = sm("kvt_sb", [128, NC_ * H], F32)
    rpt = sm("rpt", [128, NC_, H], F32)


    xT_bf = at("xT_bf", [128, NC_, S], BF16, S2)
    woT_bf = at("woT_bf", [128, NC_, DM], BF16, S2)
    k_tok = at("k_tok", [128, NT, DM], BF16, S1)
    v_tok = at("v_tok", [128, NT, DM], BF16, S1)
    aT = at("aT", [128, H, S], BF16, S1)

    kvA = at("kvA", [H, S], F32, S3)
    kvB = at("kvB", [H, S], F32, S3 + 8 * 1024)
    kvC = at("kvC", [H, S], F32, S3 + 16 * 1024)
    kselT = at("kselT", [128, H, 128], BF16, S3 + 24 * 1024)
    vsel = at("vsel", [128, H, 128], BF16, S3 + 28 * 1024)
    qT_hb = at("qT_hb", [128, NT, 128], BF16, S4 + 8 * 1024)
    attn_b = at("attn_b", [128, 2, 4, 128], BF16, S4 + 12 * 1024)
    attnn_b = at("attnn_b", [128, 2, 4, 128], BF16, S4 + 14 * 1024)
    attnt_b = at("attnt_b", [128, 2, 4, 128], BF16, S4 + 16 * 1024)
    xtok_t = at("xtok_t", [128, 2, DM], F32, S3)
    u1_t = at("u1_t", [128, DM], F32, S3 + 16 * 1024)
    y1a_t = at("y1a_t", [128, DM], F32, S3 + 24 * 1024)
    y1b_t = at("y1b_t", [128, DM], F32, S4)
    y1q = at("y1q", [128, 4, DM], F32, S3)
    y1b16 = [at("y1b16a", [128, DM], BF16, S4 + 16 * 1024),
             at("y1b16b", [128, DM], BF16, S4 + 20 * 1024)]
    u1_t2 = at("u1b_t", [128, DM], F32, S4 + 24 * 1024)
    pstg = [at("pstg0", [128, 2048], BF16, S4 + 32 * 1024),
            at("pstg1", [128, 2048], BF16, S4 + 36 * 1024)]
    sqscr = at("sqscr", [128, DM], F32, S4 + 28 * 1024)

    wslab = [at("wslab0", [128, NC_, 512], BF16, S4),
             at("wslab1", [128, NC_, 512], BF16, S4 + 16 * 1024)]
    kT_h = at("kT_h", [128, NT, 128], BF16, S3)
    h1sT = at("h1sT", [128, NT, 128], BF16, S3 + 4 * 1024)
    mt_b = [at("mt_b0", [128, NC_, 128], BF16, S3 + 8 * 1024),
            at("mt_b1", [128, NC_, 128], BF16, S3 + 12 * 1024)]
    qld = [at("qld0", [128, NT, 128], BF16, S3 + 8 * 1024),
           at("qld1", [128, NT, 128], BF16, S3 + 12 * 1024)]
    ksel_tmp = at("ksel_tmp", [128, 128], BF16, S3 + 16 * 1024)
    wqsl = [at("wqsl0", [128, NC_, 128], BF16, S4),
            at("wqsl1", [128, NC_, 128], BF16, S4 + 4 * 1024)]
    w1sl = [at("w1sl0", [128, NC_, 128], BF16, S4),
            at("w1sl1", [128, NC_, 128], BF16, S4 + 4 * 1024)]
    w2sl = [at("w2sl0", [128, NF1, 128], BF16, S4 + 8 * 1024),
            at("w2sl1", [128, NF1, 128], BF16, S4 + 16 * 1024)]
    b3_s = at("b3_s", [1, DM], BF16, S4 + 24 * 1024)

    y1T_q = [at("y1T_qa", [128, NC_, QS], BF16, S2),
             at("y1T_qb", [128, NC_, QS], BF16, S2 + 16 * 1024)]
    h1T_q = at("h1T_q", [128, NF1, QS], BF16, S2 + 32 * 1024)
    h2T_q = at("h2T_q", [128, NF1, QS], BF16, S1)
    w3sl = [at("w3sl0", [128, NF1, 512], BF16, S1 + 32 * 1024),
            at("w3sl1", [128, NF1, 512], BF16, S2 + 32 * 1024)]


    pf = [nc.alloc_psum_tensor("pf%d" % i, [128, 512], F32) for i in range(4)]
    pb = [nc.alloc_psum_tensor("pb%d" % i, [128, 512], BF16) for i in range(3)]
    pkv = nc.alloc_psum_tensor("pkv", [128, NC_ * H], F32)

    nc.const_aps.register(F32, 0.0, zcol[:, :])
    p = Plan(nc)
    ps_free = {t.name: None for t in pf + pb + [pkv]}
    ps_ready = {}

    def mmgroup(psum_name, ops, deps=()):
        d = list(deps)
        if ps_free[psum_name] is not None:
            d.append(ps_free[psum_name])
        for fn in ops[:-1]:
            p.emit("T", fn, deps=tuple(d))
            d = ()
        tokn = p.chain("T", ops[-1], deps=tuple(d))
        ps_ready[psum_name] = tokn
        return tokn

    def evac(eng, fn, psum_name, deps=()):
        d = list(deps) + [ps_ready[psum_name]]
        tokn = p.chain(eng, fn, deps=d)
        ps_free[psum_name] = tokn
        return tokn

    def transp(pn_ap, pn_name, fn, deps=()):
        d = list(deps)
        if ps_free[pn_name] is not None:
            d.append(ps_free[pn_name])
        tokn = p.chain("T", fn, deps=d)
        ps_ready[pn_name] = tokn
        return tokn

    def gdma(dst, src, deps=(), sname="GD"):
        return p.emit("G", (lambda g, dst=dst, src=src: g.dma_start(dst, src)),
                      deps=deps, inc=(sname, 16))

    def qdma(dst, src, deps=(), sname="QD"):
        return p.emit("Q", (lambda q, dst=dst, src=src: q.dma_start(dst, src)),
                      deps=deps, inc=(sname, 16))

    def vo(fn, deps=()):
        return p.chain("V", fn, deps=deps)

    def so(fn, deps=()):
        return p.chain("S", fn, deps=deps)

    def gop(fn, deps=()):
        return p.chain("G", fn, deps=deps)

    for i in range(4):
        gdma(xT_bf[:, 4 * i:4 * i + 4, :],
             xt_h[4 * i:4 * i + 4, :, :].transpose([1, 0, 2]), sname="XT")
    XT_TOK = ("XT", p.cnt["XT"])
    gdma(sw1_b[:, :], sw1T_h[:, :], sname="SML")
    gdma(sw2_b[:, :], sw2_h[:, :], sname="SML")
    gdma(sb1_s[:, :], sb1_h[:, :], sname="SML")
    gdma(sb2_s[:, :], sb2_h[:, :], sname="SML")
    gdma(b1_s[:, :], b1_h[:, :], sname="SML")
    gdma(b2_s[:, :], b2_h[:, :], sname="SML")
    SML_TOK = ("SML", p.cnt["SML"])

    gop(lambda g: g.memset(ident_f[:, :], 0.0))
    gop(lambda g: g.memset(ident_b[:, :], 0.0))
    gop(lambda g: g.memset(zcol[:, :], 0.0))
    gop(lambda g: g.memset(eps_c[:, :], EPS))
    gop(lambda g: g.memset(ones_b[:, :], 1.0))
    gop(lambda g: g.affine_select(out=ident_f[:, :], in_=ident_f[:, :],
                                  compare_op=ALU.not_equal, fill=1.0, base=0,
                                  pattern=[[-1, 128]], channel_multiplier=1))
    gop(lambda g: g.affine_select(out=ident_b[:, :], in_=ident_b[:, :],
                                  compare_op=ALU.not_equal, fill=1.0, base=0,
                                  pattern=[[-1, 128]], channel_multiplier=1))
    SETUP_TOK = gop(lambda g: g.iota(jrow[:, :], pattern=[[1, 128]], base=1,
                                     channel_multiplier=0,
                                     allow_small_or_imprecise_dtypes=True))

    prep_units = []
    for f1 in range(NF1):
        prep_units.append(("w1", f1, 0))
    for f2 in range(NF1):
        prep_units.append(("w2", f2, 0))
        prep_units.append(("w2", f2, 1))
    for f3 in range(4):
        for i in range(8):
            prep_units.append(("w3", f3, i))
    prep_state = {"i": 0, "war": [None, None]}

    def prep_pump(n):
        for _ in range(n):
            i = prep_state["i"]
            if i >= len(prep_units):
                return
            prep_state["i"] = i + 1
            kind, a, b = prep_units[i]
            buf = i % 2
            st = pstg[buf]
            d = []
            if prep_state["war"][buf] is not None:
                d.append(prep_state["war"][buf])
            if kind == "w1":
                gdma(st[:, :], w1L_h[a].rearrange("p c m -> p (c m)"),
                     deps=tuple(d), sname="PG%d" % buf)
            elif kind == "w2":
                gdma(st[:, :],
                     w2L_h[a, :, 16 * b:16 * b + 16, :].rearrange("p c m -> p (c m)"),
                     deps=tuple(d), sname="PG%d" % buf)
            else:
                gdma(st[:, :].rearrange("p (c x) -> p c x", c=4),
                     w3T_h[4 * b:4 * b + 4, :, 512 * a:512 * (a + 1)]
                     .transpose([1, 0, 2]),
                     deps=tuple(d), sname="PG%d" % buf)
            gtok = ("PG%d" % buf, p.cnt["PG%d" % buf])
            if kind == "w1":
                qdma(w1scr[a], st[:, :], deps=[gtok], sname="PQ%d" % buf)
            elif kind == "w2":
                qdma(w2scr[a, :, 2048 * b:2048 * (b + 1)], st[:, :], deps=[gtok],
                     sname="PQ%d" % buf)
            else:
                qdma(w3scr[a, :, 4 * b:4 * b + 4, :],
                     st[:, :].rearrange("p (c x) -> p c x", c=4), deps=[gtok],
                     sname="PQ%d" % buf)
            prep_state["war"][buf] = ("PQ%d" % buf, p.cnt["PQ%d" % buf])

    slab_war = [None, None]
    slab_tok = [None, None]

    def proj_token_major(w_dram, dest, base_deps, s_only=False, after_fb=None):
        for fb in range(4):
            buf = fb % 2
            d = [SETUP_TOK] + list(base_deps)
            if slab_war[buf] is not None:
                d.append(slab_war[buf])
            for piece in range(2):
                gdma(wslab[buf][:, 8 * piece:8 * piece + 8, :],
                     w_dram[8 * piece:8 * piece + 8, :, 512 * fb:512 * (fb + 1)]
                     .transpose([1, 0, 2]),
                     deps=tuple(d), sname="WS%d" % buf)
                d = []
            slab_tok[buf] = ("WS%d" % buf, p.cnt["WS%d" % buf])
            for ts in range(NT):
                pn = pf[ts % 2]
                ops = []
                for dc in range(NC_):
                    def mk(t, dc=dc, ts=ts, pn=pn, buf=buf):
                        return t.matmul(pn[:, :],
                                        xT_bf[:, dc, 128 * ts:128 * (ts + 1)],
                                        wslab[buf][:, dc, :],
                                        start=(dc == 0), stop=(dc == NC_ - 1))
                    ops.append(mk)
                mmgroup(pn.name, ops, deps=[XT_TOK, slab_tok[buf]])
                eng = "S" if s_only else ("V" if ts % 2 == 0 else "S")
                if eng == "V":
                    def ev(e, ts=ts, fb=fb, pn=pn):
                        return e.tensor_copy(dest[:, ts, 512 * fb:512 * (fb + 1)],
                                             pn[:, :])
                else:
                    def ev(e, ts=ts, fb=fb, pn=pn):
                        return e.copy(dest[:, ts, 512 * fb:512 * (fb + 1)], pn[:, :])
                evac(eng, ev, pn.name)
            slab_war[buf] = p.tok("T")
            prep_pump(8)
            if after_fb is not None:
                after_fb(fb, p.tok("V"), p.tok("S"))

    kv_state = {"first": True}

    def kv_head_pipeline(h, vtok, stok):
        for ts in range(NT):
            pbn = pb[ts % 2]
            def tr(t, ts=ts, h=h, pbn=pbn):
                return t.transpose(pbn[:, 0:128],
                                   k_tok[:, ts, 128 * h:128 * (h + 1)],
                                   ident_b[:, :])
            transp(pbn, pbn.name, tr, deps=[SETUP_TOK, vtok, stok])
            evac("V", (lambda e, ts=ts, pbn=pbn:
                       e.tensor_copy(kT_h[:, ts, :], pbn[:, 0:128])), pbn.name)
        kT_tok = p.tok("V")
        for tb in range(4):
            pn = pf[2 + tb % 2]
            def mm(t, tb=tb, pn=pn):
                return t.matmul(pn[:, :], sw1_b[:, :], kT_h[:, 4 * tb:4 * tb + 4, :],
                                start=True, stop=True)
            mmgroup(pn.name, [mm], deps=[kT_tok, SML_TOK])
            def ev(e, tb=tb, pn=pn):
                return e.activation(h1sT[:, 4 * tb:4 * tb + 4, :], pn[:, :],
                                    AF.Relu, bias=sb1_s[:, :], scale=1.0)
            evac("S", ev, pn.name)
        h1_tok = p.tok("S")
        prep_pump(3)
        d = [h1_tok, SML_TOK]
        for c in range(NC_):
            first = kv_state["first"]
            kv_state["first"] = False
            def mk(t, c=c, h=h, first=first):
                return t.matmul(pkv[:, c * H + h:c * H + h + 1],
                                h1sT[:, c, :], sw2_b[:, :],
                                start=first, stop=False)
            p.emit("T", mk, deps=tuple(d))
            d = []

    def p1_after_fb(fb, vtok, stok):
        for h in range(4 * fb, 4 * fb + 4):
            kv_head_pipeline(h, vtok, stok)

    proj_token_major(wkT_h, k_tok, [], after_fb=p1_after_fb)
    for c in range(NC_):
        def mk(t, c=c):
            return t.matmul(pkv[:, H * c:H * (c + 1)], ones_b[:, :], sb2_s[0:1, :],
                            start=False, stop=(c == NC_ - 1))
        if c == NC_ - 1:
            tokn = p.chain("T", mk, deps=[SML_TOK])
            ps_ready[pkv.name] = tokn
        else:
            p.emit("T", mk, deps=[SML_TOK])
    evac("V", lambda e: e.tensor_copy(kvt_sb[:, :], pkv[:, :]), pkv.name)
    kvt_tok = p.tok("V")
    for c in range(NC_):
        pn = pf[c % 2]
        def tr(t, c=c, pn=pn):
            return t.transpose(pn[0:16, 0:128], kvt_sb[:, H * c:H * (c + 1)],
                               ident_f[:, :])
        transp(pn, pn.name, tr, deps=[kvt_tok])
        evac("V", (lambda e, c=c, pn=pn:
                   e.tensor_copy(kvA[:, 128 * c:128 * (c + 1)], pn[0:16, 0:128])),
             pn.name)

    wq_war = [None, None]
    qstg_war = None
    for h in range(H):
        buf = h % 2
        d = [SETUP_TOK, slab_war[0], slab_war[1]]
        if wq_war[buf] is not None:
            d.append(wq_war[buf])
        gdma(wqsl[buf][:, :, :], wqL_h[h], deps=tuple(d), sname="WQ%d" % buf)
        wq_tok = ("WQ%d" % buf, p.cnt["WQ%d" % buf])
        for tb in range(4):
            pn = pf[tb % 2]
            ops = []
            for dc in range(NC_):
                def mk(t, dc=dc, tb=tb, pn=pn, buf=buf):
                    return t.matmul(pn[:, :], wqsl[buf][:, dc, :],
                                    xT_bf[:, dc, 512 * tb:512 * (tb + 1)],
                                    start=(dc == 0), stop=(dc == NC_ - 1))
                ops.append(mk)
            mmgroup(pn.name, ops, deps=[XT_TOK, wq_tok])
            dd = [qstg_war] if (qstg_war is not None and tb == 0) else []
            def ev(e, tb=tb, pn=pn):
                return e.activation(qT_hb[:, 4 * tb:4 * tb + 4, :], pn[:, :],
                                    AF.Copy, bias=0.0, scale=SCALE)
            evac("S", ev, pn.name, deps=dd + [SML_TOK])
        wq_war[buf] = p.tok("T")
        q_stok = p.tok("S")
        qdma(qTscr[h], qT_hb[:, :, :].rearrange("p c m -> p (c m)"),
             deps=[q_stok], sname="QS")
        qstg_war = ("QS", p.cnt["QS"])
    QS_ALL = ("QS", p.cnt["QS"])

    vo(lambda v: v.tensor_reduce(lo_c[:, :], kvA[:, :], axis=AX.X, op=ALU.min))
    vo(lambda v: v.tensor_reduce(hi_c[:, :], kvA[:, :], axis=AX.X, op=ALU.max))
    vo(lambda v: v.tensor_scalar_add(lo_c[:, :], lo_c[:, :], -1.0))
    vo(lambda v: v.tensor_scalar_add(hi_c[:, :], hi_c[:, :], 1.0))
    for it in range(NBIN):
        vo(lambda v: v.tensor_add(mid_c[:, :], lo_c[:, :], hi_c[:, :]))
        vo(lambda v: v.tensor_scalar_mul(mids_c[:, :], mid_c[:, :], 0.5))
        vo(lambda v: v.tensor_scalar(kvB[:, :], kvA[:, :], mids_c[:, :], None,
                                     op0=ALU.is_ge, op1=ALU.add,
                                     accum_out=cnt_c[:, :]))
        vo(lambda v: v.tensor_scalar(sel_c[:, :], cnt_c[:, :], float(KSEL), None,
                                     op0=ALU.is_ge))
        vo(lambda v: v.tensor_sub(d0_c[:, :], mids_c[:, :], lo_c[:, :]))
        vo(lambda v: v.scalar_tensor_tensor(lo_c[:, :], d0_c[:, :], sel_c[:, :],
                                            lo_c[:, :], op0=ALU.mult,
                                            op1=ALU.add))
        vo(lambda v: v.tensor_sub(d0_c[:, :], hi_c[:, :], mids_c[:, :]))
        vo(lambda v: v.scalar_tensor_tensor(hi_c[:, :], d0_c[:, :], sel_c[:, :],
                                            mids_c[:, :], op0=ALU.mult,
                                            op1=ALU.add))
    vo(lambda v: v.tensor_scalar(kvB[:, :], kvA[:, :], lo_c[:, :], None,
                                 op0=ALU.is_ge))
    vo(lambda v: v.tensor_tensor_scan(kvC[:, :], kvB[:, :], kvB[:, :], 0.0,
                                      op0=ALU.add, op1=ALU.bypass))
    RP_TOK = vo(lambda v: v.tensor_mul(kvA[:, :], kvC[:, :], kvB[:, :]))
    for c in range(NC_):
        pn = pf[2 + c % 2]
        def tr(t, c=c, pn=pn):
            return t.transpose(pn[0:128, 0:16], kvA[:, 128 * c:128 * (c + 1)],
                               ident_f[0:16, 0:16])
        transp(pn, pn.name, tr, deps=[RP_TOK])
        evac("V", (lambda e, c=c, pn=pn:
                   e.tensor_copy(rpt[:, c, :], pn[0:128, 0:16])), pn.name)
    RPT_TOK = p.tok("V")

    gather_war = [None, None]

    def gather_mt(h, extra_dep):
        buf = h % 2
        d = [RPT_TOK, extra_dep]
        if gather_war[buf] is not None:
            d.append(gather_war[buf])
        vo((lambda v, h=h, buf=buf: v.tensor_tensor(
            mt_b[buf][:, :, :],
            jrow[:, :].unsqueeze(1).to_broadcast([128, NC_, 128]),
            rpt[:, :, h:h + 1].to_broadcast([128, NC_, 128]),
            op=ALU.is_equal)), deps=d)
        return p.tok("V")

    def gather_mm(dest_sel, src_tok, h, mt_tok, src_deps, transpose_out=False):
        buf = h % 2
        pn = pf[3]
        ops = []
        for c in range(NC_):
            def mk(t, c=c, h=h, pn=pn, buf=buf):
                return t.matmul(pn[:, 0:128], mt_b[buf][:, c, :],
                                src_tok[:, c, 128 * h:128 * (h + 1)],
                                start=(c == 0), stop=(c == NC_ - 1))
            ops.append(mk)
        mmgroup(pn.name, ops, deps=[mt_tok] + list(src_deps))
        gather_war[buf] = p.tok("T")
        if not transpose_out:
            evac("V", (lambda e, h=h: e.tensor_copy(dest_sel[:, h, :],
                                                    pn[:, 0:128])), pn.name)
        else:
            ev_tok = evac("V", (lambda e: e.tensor_copy(ksel_tmp[:, :],
                                                        pn[:, 0:128])), pn.name)
            pbn = pb[0]
            def tr(t, pbn=pbn):
                return t.transpose(pbn[:, 0:128], ksel_tmp[:, :], ident_b[:, :])
            transp(pbn, pbn.name, tr, deps=[ev_tok])
            evac("V", (lambda e, h=h: e.tensor_copy(dest_sel[:, h, :],
                                                    pbn[:, 0:128])), pbn.name)

    def gather_batch(dest_sel, src_tok, heads, extra_dep, src_deps,
                     transpose_out=False):
        mt_toks = {}
        pending = []
        for h in heads:
            mt_toks[h] = gather_mt(h, extra_dep)
            pending.append(h)
            if len(pending) >= 2:
                h0 = pending.pop(0)
                gather_mm(dest_sel, src_tok, h0, mt_toks[h0], src_deps,
                          transpose_out)
        for h0 in pending:
            gather_mm(dest_sel, src_tok, h0, mt_toks[h0], src_deps, transpose_out)


    KG_V = p.tok("V")
    KG_S = p.tok("S")
    gather_batch(kselT, k_tok, range(H), SETUP_TOK, [KG_V, KG_S],
                 transpose_out=True)
    KSEL_DONE = p.tok("V")

    def p3_after_fb(fb, vtok, stok):
        gather_batch(vsel, v_tok, range(4 * fb, 4 * fb + 4), SETUP_TOK,
                     [vtok, stok])

    proj_token_major(wvT_h, v_tok, [p.tok("T"), QS_ALL], s_only=True,
                     after_fb=p3_after_fb)
    VSEL_DONE = p.tok("V")

    P5_T0 = p.tok("T")
    qld_war = [None, None]
    wo_pieces = [0]
    for h in range(H):
        buf = h % 2
        d = [QS_ALL]
        if qld_war[buf] is not None:
            d.append(qld_war[buf])
        else:
            d.append(P5_T0)
        gdma(qld[buf][:, :, :].rearrange("p c m -> p (c m)"), qTscr[h],
             deps=tuple(d), sname="QL%d" % buf)
        q_tok = ("QL%d" % buf, p.cnt["QL%d" % buf])
        qT_h = qld[buf]
        prep_pump(2)
        def attn_scores(qsb):
            pn = pf[2 + qsb % 2]
            ops = []
            for j in range(4):
                def mk(t, j=j, qsb=qsb, h=h, pn=pn, qT_h=qT_h):
                    return t.matmul(pn[:, 128 * j:128 * (j + 1)],
                                    qT_h[:, 4 * qsb + j, :], kselT[:, h, :],
                                    start=(j == 0), stop=(j == 3))
                ops.append(mk)
            mmgroup(pn.name, ops, deps=[q_tok, KSEL_DONE])
            return pn

        def attn_maxnm(qsb, pn):
            pi = qsb % 2
            return vo((lambda v, pn=pn, pi=pi: v.reduce_max(
                nm4[:, pi, :], pn[:, :].rearrange("p (j x) -> p j x", j=4),
                axis=AX.X, negate=True)), deps=[ps_ready[pn.name]])

        def attn_exp(qsb, pn, nm_tok):
            pi = qsb % 2
            ex_tok = None
            for j in range(4):
                def ex(e, j=j, pn=pn, pi=pi):
                    return e.activation(attn_b[:, pi, j, :],
                                        pn[:, 128 * j:128 * (j + 1)],
                                        AF.Exp, bias=nm4[:, pi, j:j + 1],
                                        scale=1.0, accum_out=se4[:, pi, j:j + 1])
                ex_tok = so(ex, deps=[nm_tok] if j == 0 else [])
            ps_free[pn.name] = ex_tok
            return ex_tok

        def attn_norm(qsb, ex_tok):
            pi = qsb % 2
            vo((lambda v, pi=pi: v.reciprocal(rc4[:, pi, :], se4[:, pi, :])),
               deps=[ex_tok])
            for j in range(4):
                vo((lambda v, j=j, pi=pi: v.tensor_scalar_mul(
                    attnn_b[:, pi, j, :], attn_b[:, pi, j, :],
                    rc4[:, pi, j:j + 1])))
            return p.tok("V")

        def attn_trcopy(qsb, an_tok, banks):
            pi = qsb % 2
            at_tok = None
            for j in range(4):
                pbn = banks[j % 2]
                def tr(t, j=j, pbn=pbn, pi=pi):
                    return t.transpose(pbn[:, 128 * (j // 2):128 * (j // 2 + 1)],
                                       attnn_b[:, pi, j, :], ident_b[:, :])
                d = [an_tok]
                if ps_free[pbn.name] is not None:
                    d.append(ps_free[pbn.name])
                ttok = p.chain("T", tr, deps=d)
                ps_ready[pbn.name] = ttok
                def cpy(e, j=j, pbn=pbn, pi=pi):
                    return e.copy(attnt_b[:, pi, j, :],
                                  pbn[:, 128 * (j // 2):128 * (j // 2 + 1)])
                at_tok = so(cpy, deps=[ttok])
                ps_free[pbn.name] = at_tok
            return at_tok

        def attn_av(qsb, at_tok):
            pi = qsb % 2
            pn2 = pf[qsb % 2]
            ops2 = []
            for j in range(4):
                def av(t, j=j, h=h, pn2=pn2, pi=pi):
                    return t.matmul(pn2[:, 128 * j:128 * (j + 1)], vsel[:, h, :],
                                    attnt_b[:, pi, j, :], start=(j == 0),
                                    stop=(j == 3))
                ops2.append(av)
            mmgroup(pn2.name, ops2, deps=[at_tok, VSEL_DONE])
            evac("V", (lambda e, h=h, qsb=qsb, pn2=pn2:
                       e.tensor_copy(aT[:, h, 512 * qsb:512 * (qsb + 1)],
                                     pn2[:, :])), pn2.name)

        for pr in range(2):
            A, B = 2 * pr, 2 * pr + 1
            pnA = attn_scores(A)
            pnB = attn_scores(B)
            nmA = attn_maxnm(A, pnA)
            exA = attn_exp(A, pnA, nmA)
            nmB = attn_maxnm(B, pnB)
            rcA = attn_norm(A, exA)
            exB = attn_exp(B, pnB, nmB)
            atA = attn_trcopy(A, rcA, [pb[0], pb[1]])
            rcB = attn_norm(B, exB)
            atB = attn_trcopy(B, rcB, [pb[2], pb[1]])
            attn_av(A, atA)
            attn_av(B, atB)
        qld_war[buf] = p.tok("T")
    AT_DONE = p.tok("V")

    prep_pump(len(prep_units))
    PREP_DONE0 = ("PQ0", p.cnt["PQ0"])
    PREP_DONE1 = ("PQ1", p.cnt["PQ1"])
    T_AFTER_Q = p.tok("T")
    for i in range(4):
        gdma(woT_bf[:, 4 * i:4 * i + 4, :],
             woT_h[4 * i:4 * i + 4, :, :].transpose([1, 0, 2]),
             deps=([P5_T0] if i == 0 else ()), sname="WO")
    WO_TOK = ("WO", p.cnt["WO"])
    xt_war = [None, None]
    y1_war = [None, None]
    stg_war = [[PREP_DONE0, PREP_DONE1], [PREP_DONE0, PREP_DONE1]]
    pend_y1T = []
    for ts in range(NT):
        buf = ts % 2
        u1 = u1_t if buf == 0 else u1_t2
        d = [AT_DONE]
        if xt_war[buf] is not None:
            d.append(xt_war[buf])
        qdma(xtok_t[:, buf, :], xtok_h[128 * ts:128 * (ts + 1), :], deps=tuple(d),
             sname="XK%d" % buf)
        xk_tok = ("XK%d" % buf, p.cnt["XK%d" % buf])
        for fb in range(4):
            pn = pf[fb % 4]
            ops = []
            for hh in range(H):
                def mk(t, hh=hh, ts=ts, fb=fb, pn=pn):
                    return t.matmul(pn[:, :], aT[:, hh, 128 * ts:128 * (ts + 1)],
                                    woT_bf[:, hh, 512 * fb:512 * (fb + 1)],
                                    start=(hh == 0), stop=(hh == H - 1))
                ops.append(mk)
            mmgroup(pn.name, ops, deps=[WO_TOK, AT_DONE])
            def ev(e, fb=fb, pn=pn, buf=buf, u1=u1):
                return e.tensor_tensor(u1[:, 512 * fb:512 * (fb + 1)], pn[:, :],
                                       xtok_t[:, buf, 512 * fb:512 * (fb + 1)],
                                       op=ALU.add)
            tokn = evac("V", ev, pn.name, deps=[xk_tok])
            if fb == 3:
                xt_war[buf] = tokn
        if pend_y1T:
            pend_y1T.pop(0)()
        U1_TOK = p.tok("V")
        y1t = y1a_t if buf == 0 else y1b_t
        ysem = "YW%d" % buf
        d_sq = [U1_TOK, SML_TOK]
        if y1_war[buf] is not None:
            d_sq.append(y1_war[buf])
        vo((lambda v, u1=u1, buf=buf: v.reduce_sum(mean_c[:, buf:buf + 1],
                                                   u1[:, :], axis=AX.X)))
        MN_TOK = vo((lambda v, buf=buf: v.tensor_scalar_mul(
            mean_c[:, buf:buf + 1], mean_c[:, buf:buf + 1], 1.0 / DM)))
        SQ_TOK = so((lambda s, y1t=y1t, u1=u1, buf=buf: s.activation(
            y1t[:, :], u1[:, :], AF.Square, bias=zcol[:, :], scale=1.0,
            accum_out=sq_c[:, buf:buf + 1])), deps=d_sq)
        vo((lambda v, buf=buf: v.tensor_scalar_mul(
            var_c[:, buf:buf + 1], sq_c[:, buf:buf + 1], 1.0 / DM)),
           deps=[SQ_TOK])
        vo((lambda v, buf=buf: v.tensor_mul(sd_c[:, buf:buf + 1],
                                            mean_c[:, buf:buf + 1],
                                            mean_c[:, buf:buf + 1])))
        VV_TOK = vo((lambda v, buf=buf: v.tensor_sub(var_c[:, buf:buf + 1],
                                                     var_c[:, buf:buf + 1],
                                                     sd_c[:, buf:buf + 1])))
        SD_TOK = so((lambda s, buf=buf: s.activation(
            sd_c[:, buf:buf + 1], var_c[:, buf:buf + 1], AF.Sqrt,
            bias=eps_c[:, :], scale=1.0)), deps=[VV_TOK])
        RS_TOK = vo((lambda v, buf=buf: v.reciprocal(rs_c[:, buf:buf + 1],
                                                     sd_c[:, buf:buf + 1])),
                    deps=[SD_TOK])
        Y1_TOK = vo((lambda v, y1t=y1t, u1=u1, buf=buf: v.tensor_scalar(
            y1t[:, :], u1[:, :], mean_c[:, buf:buf + 1], rs_c[:, buf:buf + 1],
            op0=ALU.subtract, op1=ALU.mult)), deps=[RS_TOK, MN_TOK])
        qdma(y1s_h[ts, :, :], y1t[:, :], deps=[Y1_TOK], sname=ysem)
        y1_war[buf] = (ysem, p.cnt[ysem])
        yb = y1b16[buf]
        Y1B_TOK = so((lambda s, y1t=y1t, yb=yb: s.copy(yb[:, :], y1t[:, :])),
                     deps=[Y1_TOK])

        def do_y1T(ts=ts, buf=buf, yb=yb, Y1B_TOK=Y1B_TOK):
            stg = pstg[buf]
            ev_toks = []
            for dc in range(NC_):
                pbn = pb[dc % 3]
                def tr(t, dc=dc, pbn=pbn, yb=yb):
                    return t.transpose(pbn[:, 0:128],
                                       yb[:, 128 * dc:128 * (dc + 1)],
                                       ident_b[:, :])
                transp(pbn, pbn.name, tr, deps=[Y1B_TOK, SETUP_TOK])
                eng = "V" if dc % 2 == 0 else "S"
                dd = list(stg_war[buf]) if dc < 2 else []
                if eng == "V":
                    def ev2(e, dc=dc, pbn=pbn, stg=stg):
                        return e.tensor_copy(stg[:, 128 * dc:128 * (dc + 1)],
                                             pbn[:, 0:128])
                else:
                    def ev2(e, dc=dc, pbn=pbn, stg=stg):
                        return e.copy(stg[:, 128 * dc:128 * (dc + 1)],
                                      pbn[:, 0:128])
                ev_toks.append(evac(eng, ev2, pbn.name, deps=dd))
            qdma(y1Tscr[ts // 4, :, :, ts % 4, :],
                 stg[:, :].rearrange("p (c x) -> p c x", c=NC_),
                 deps=[ev_toks[-1], ev_toks[-2]], sname="YT%d" % buf)
            stg_war[buf] = [("YT%d" % buf, p.cnt["YT%d" % buf])]
        pend_y1T.append(do_y1T)
    for fn_ in pend_y1T:
        fn_()
    pend_y1T = []
    Y1W0 = ("YW0", p.cnt["YW0"])
    Y1W1 = ("YW1", p.cnt["YW1"])
    YT0 = ("YT0", p.cnt["YT0"])
    YT1 = ("YT1", p.cnt["YT1"])

    P7_T0 = p.tok("T")
    gdma(b3_s[:, :], b3_h[:, :], deps=[P7_T0], sname="SMX")
    B3_TOK = ("SMX", p.cnt["SMX"])
    w1_war = [None, None]
    w2_war = [None, None]
    w3_war = [None, None]
    y1tq_war = [None, None]
    ln2_prev = None
    yo_prev = []
    def yq_prefetch(qq):
        bufy = qq % 2
        d = [YT0, YT1]
        if y1tq_war[bufy] is not None:
            d.append(y1tq_war[bufy])
        qdma(y1T_q[bufy][:, :, :], y1Tscr[qq].rearrange("p c s x -> p c (s x)"),
             deps=tuple(d), sname="YQ%d" % bufy)
        return ("YQ%d" % bufy, p.cnt["YQ%d" % bufy])

    yq_next = yq_prefetch(0)
    for qq in range(QT):
        bufy = qq % 2
        yq_tok = yq_next
        for f1 in range(NF1):
            buf = f1 % 2
            if qq > 0 and f1 < 2:
                w1_tok = w1_pre[f1]
            else:
                d = [PREP_DONE0, PREP_DONE1]
                d.append(w1_war[buf] if w1_war[buf] is not None else Y1W0)
                if w1_war[buf] is None:
                    d.append(Y1W1)
                gdma(w1sl[buf][:, :, :].rearrange("p c m -> p (c m)"), w1scr[f1],
                     deps=tuple(d), sname="W1%d" % buf)
                w1_tok = ("W1%d" % buf, p.cnt["W1%d" % buf])
            pn = pf[f1 % 2]
            ops = []
            for dc in range(NC_):
                def mk(t, dc=dc, pn=pn, buf=buf, bufy=bufy):
                    return t.matmul(pn[:, :], w1sl[buf][:, dc, :],
                                    y1T_q[bufy][:, dc, :], start=(dc == 0),
                                    stop=(dc == NC_ - 1))
                ops.append(mk)
            mmgroup(pn.name, ops, deps=[w1_tok, yq_tok])
            w1_war[buf] = p.tok("T")
            def ev(e, f1=f1, pn=pn):
                return e.activation(h1T_q[:, f1, :], pn[:, :], AF.Relu,
                                    bias=b1_s[:, f1:f1 + 1], scale=1.0)
            evac("S", ev, pn.name, deps=[SML_TOK])
        y1tq_war[bufy] = p.tok("T")
        H1_TOK = p.tok("S")
        for f2 in range(NF1):
            buf = f2 % 2
            d = [PREP_DONE0, PREP_DONE1]
            d.append(w2_war[buf] if w2_war[buf] is not None else P7_T0)
            qdma(w2sl[buf][:, :, :].rearrange("p c m -> p (c m)"), w2scr[f2],
                 deps=tuple(d), sname="W2%d" % buf)
            w2_tok = ("W2%d" % buf, p.cnt["W2%d" % buf])
            pn = pf[f2 % 2]
            ops = []
            for fc in range(NF1):
                def mk(t, fc=fc, pn=pn, buf=buf):
                    return t.matmul(pn[:, :], w2sl[buf][:, fc, :],
                                    h1T_q[:, fc, :], start=(fc == 0),
                                    stop=(fc == NF1 - 1))
                ops.append(mk)
            mmgroup(pn.name, ops, deps=[w2_tok, H1_TOK])
            w2_war[buf] = p.tok("T")
            def ev(e, f2=f2, pn=pn):
                return e.activation(h2T_q[:, f2, :], pn[:, :], AF.Relu,
                                    bias=b2_s[:, f2:f2 + 1], scale=1.0)
            evac("S", ev, pn.name, deps=[SML_TOK])
        H2_TOK = p.tok("S")
        if qq + 1 < QT:
            yq_next = yq_prefetch(qq + 1)
            w1_pre = []
            for f1 in range(2):
                d = [PREP_DONE0, PREP_DONE1, w1_war[f1 % 2]]
                gdma(w1sl[f1 % 2][:, :, :].rearrange("p c m -> p (c m)"),
                     w1scr[f1], deps=tuple(d), sname="W1%d" % (f1 % 2))
                w1_pre.append(("W1%d" % (f1 % 2), p.cnt["W1%d" % (f1 % 2)]))
        d = [Y1W0, Y1W1] + yo_prev
        if ln2_prev is not None:
            d.append(ln2_prev)
        qdma(y1q[:, :, :], y1s_h[4 * qq:4 * qq + 4, :, :].transpose([1, 0, 2]),
             deps=tuple(d), sname="Y1R")
        y1r_tok = ("Y1R", p.cnt["Y1R"])
        W2_T = p.tok("T")
        ev_last = [None] * 4
        for f3 in range(4):
            wb = w3sl[f3 % 2]
            d = [PREP_DONE0, PREP_DONE1]
            if f3 % 2 == 1:
                d.append(W2_T)
            d.append(w3_war[f3 % 2] if w3_war[f3 % 2] is not None else T_AFTER_Q)
            gdma(wb[:, :, :], w3scr[f3], deps=tuple(d), sname="W3S%d" % (f3 % 2))
            w3_tok = ("W3S%d" % (f3 % 2), p.cnt["W3S%d" % (f3 % 2)])
            for ts in range(4):
                pn = pf[2 + ts % 2]
                ops = [lambda t, f3=f3, pn=pn: t.matmul(
                    pn[:, :], ones_b[:, :], b3_s[0:1, 512 * f3:512 * (f3 + 1)],
                    start=True, stop=False)]
                for fc in range(NF1):
                    def mk(t, fc=fc, ts=ts, pn=pn, wb=wb):
                        return t.matmul(pn[:, :],
                                        h2T_q[:, fc, 128 * ts:128 * (ts + 1)],
                                        wb[:, fc, :], start=False,
                                        stop=(fc == NF1 - 1))
                    ops.append(mk)
                mmgroup(pn.name, ops, deps=[w3_tok, B3_TOK, H2_TOK])
                def ev(e, ts=ts, f3=f3, pn=pn):
                    return e.tensor_tensor(y1q[:, ts, 512 * f3:512 * (f3 + 1)],
                                           pn[:, :],
                                           y1q[:, ts, 512 * f3:512 * (f3 + 1)],
                                           op=ALU.add)
                ev_last[ts] = evac("V", ev, pn.name, deps=[y1r_tok])
            w3_war[f3 % 2] = p.tok("T")
        for ts in range(4):
            vo((lambda v, ts=ts: v.reduce_sum(mean_c[:, 0:1], y1q[:, ts, :],
                                              axis=AX.X)), deps=[ev_last[ts]])
            MN2 = vo(lambda v: v.tensor_scalar_mul(mean_c[:, 0:1], mean_c[:, 0:1],
                                                   1.0 / DM))
            d_sq = [ev_last[ts], SML_TOK]
            if qq == 0 and ts == 0:
                d_sq += [YT0, YT1]
            SQ2 = so((lambda s, ts=ts: s.activation(
                sqscr[:, :], y1q[:, ts, :], AF.Square, bias=zcol[:, :], scale=1.0,
                accum_out=sq_c[:, 0:1])), deps=d_sq)
            vo(lambda v: v.tensor_scalar_mul(var_c[:, 0:1], sq_c[:, 0:1], 1.0 / DM),
               deps=[SQ2])
            vo(lambda v: v.tensor_mul(sd_c[:, 0:1], mean_c[:, 0:1], mean_c[:, 0:1]),
               deps=[MN2])
            VV2 = vo(lambda v: v.tensor_sub(var_c[:, 0:1], var_c[:, 0:1], sd_c[:, 0:1]))
            SD2 = so(lambda s: s.activation(sd_c[:, 0:1], var_c[:, 0:1], AF.Sqrt,
                                            bias=eps_c[:, :], scale=1.0),
                     deps=[VV2])
            RS2 = vo(lambda v: v.reciprocal(rs_c[:, 0:1], sd_c[:, 0:1]), deps=[SD2])
            Y2 = vo((lambda v, ts=ts: v.tensor_scalar(
                y1q[:, ts, :], y1q[:, ts, :], mean_c[:, 0:1], rs_c[:, 0:1],
                op0=ALU.subtract, op1=ALU.mult)), deps=[RS2])
            qdma(y_h[128 * (4 * qq + ts):128 * (4 * qq + ts + 1), :],
                 y1q[:, ts, :], deps=[Y2], sname="YO%d" % (ts % 2))
        ln2_prev = p.tok("V")
        yo_prev = [("YO0", p.cnt["YO0"]), ("YO1", p.cnt["YO1"])]

    p.emit("Q", lambda q: None, deps=[("YO0", p.cnt["YO0"]),
                                      ("YO1", p.cnt["YO1"])])

    with nc.Block() as block:
        @block.tensor
        def _(t):
            p.replay("T", t)

        @block.vector
        def _(v):
            p.replay("V", v)

        @block.scalar
        def _(s):
            p.replay("S", s)

        @block.sync
        def _(q):
            p.replay("Q", q)

        @block.gpsimd
        def _(g):
            p.replay("G", g)

    return nc


_NC_CACHE = None


def _get_nc():
    global _NC_CACHE
    if _NC_CACHE is None:
        _NC_CACHE = build_nc()
    return _NC_CACHE


def kernel(L0_attn_x, L0_attn_wq, L0_attn_wk, L0_attn_wv, L0_attn_wo,
           L0_attn_sw1, L0_attn_sb1, L0_attn_sw2, L0_attn_sb2,
           L0_ffn_W1, L0_ffn_b1, L0_ffn_W2, L0_ffn_b2, L0_ffn_W3, L0_ffn_b3):
    x = np.asarray(L0_attn_x, dtype=np.float32)
    B = x.shape[0]
    C = np.ascontiguousarray

    wq = np.asarray(L0_attn_wq, np.float32)
    wk = np.asarray(L0_attn_wk, np.float32)
    wv = np.asarray(L0_attn_wv, np.float32)
    wo = np.asarray(L0_attn_wo, np.float32)
    W1 = np.asarray(L0_ffn_W1, np.float32)
    W2 = np.asarray(L0_ffn_W2, np.float32)
    W3 = np.asarray(L0_ffn_W3, np.float32)

    shared = {
        "wkT": C(wk.T.reshape(NC_, 128, DM)),
        "wvT": C(wv.T.reshape(NC_, 128, DM)),
        "woT": C(wo.T.reshape(NC_, 128, DM)),
        "wqL": C(wq.reshape(H, 128, NC_, 128).transpose(0, 3, 2, 1)),
        "sw1T": C(np.asarray(L0_attn_sw1, np.float32).T),
        "sb1c": C(np.asarray(L0_attn_sb1, np.float32).reshape(DH, 1)),
        "sw2c": C(np.asarray(L0_attn_sw2, np.float32).reshape(1, DH).T),
        "sb2r": C(np.broadcast_to(
            np.asarray(L0_attn_sb2, np.float32).reshape(1, 1), (1, H))),
        "w1L": C(W1.reshape(NF1, 128, NC_, 128).transpose(0, 3, 2, 1)),
        "b1b": C(np.asarray(L0_ffn_b1, np.float32).reshape(NF1, 128).T),
        "w2L": C(W2.reshape(NF1, 128, NF1, 128).transpose(0, 3, 2, 1)),
        "b2b": C(np.asarray(L0_ffn_b2, np.float32).reshape(NF1, 128).T),
        "w3T": C(W3.T.reshape(NF1, 128, DM)),
        "b3r": C(np.asarray(L0_ffn_b3, np.float32).reshape(1, DM)),
    }
    in_maps = []
    for b in range(B):
        m = dict(shared)
        m["xt"] = C(x[b].T.reshape(NC_, 128, S))
        m["xtok"] = C(x[b])
        in_maps.append(m)

    nc = _get_nc()
    res = run_bass_kernel_spmd(nc, in_maps, core_ids=list(range(B)))
    return np.stack([r["y"] for r in res.results], axis=0)


# version 22 (final)
# speedup vs baseline: 1.0310x; 1.0310x over previous
import math
import numpy as np
import concourse.bass as bass
import concourse.mybir as mybir
from concourse.bass_utils import run_bass_kernel_spmd

F32 = mybir.dt.float32
BF16 = mybir.dt.bfloat16
AF = mybir.ActivationFunctionType
ALU = mybir.AluOpType
AX = mybir.AxisListType

S = 2048
DM = 2048
H = 16
DH = 128
KSEL = 128
FF = 4096
NT = S // 128
NC_ = DM // 128
NF1 = FF // 128
SCALE = 1.0 / math.sqrt(DH)
EPS = 1e-5
NBIN = 48
QT = 4
QS = S // QT


class Plan:
    ENGS = ("T", "V", "S", "Q", "G")

    def __init__(self, nc):
        self.nc = nc
        self.items = {e: [] for e in self.ENGS}
        self.sem_handles = {}
        self.cnt = {}

    def sem(self, name):
        if name not in self.sem_handles:
            self.sem_handles[name] = self.nc.alloc_semaphore(name)
            self.cnt[name] = 0
        return self.sem_handles[name]

    def emit(self, eng, fn, deps=(), inc=None):
        tok = None
        if inc is not None:
            self.sem(inc[0])
            self.cnt[inc[0]] += inc[1]
            tok = (inc[0], self.cnt[inc[0]])
        self.items[eng].append((tuple(deps), fn, inc))
        return tok

    def chain(self, eng, fn, deps=()):
        cn = "c" + eng
        self.sem(cn)
        prev = self.cnt[cn]
        d = list(deps)
        if prev > 0:
            d.append((cn, prev))
        return self.emit(eng, fn, deps=d, inc=(cn, 1))

    def tok(self, eng):
        cn = "c" + eng
        self.sem(cn)
        return (cn, self.cnt[cn])

    def replay(self, eng_name, engine):
        watermark = {}
        for deps, fn, inc in self.items[eng_name]:
            for sname, val in deps:
                if val > 0 and watermark.get(sname, 0) < val:
                    engine.wait_ge(self.sem_handles[sname], val)
                    watermark[sname] = val
            instr = fn(engine)
            if inc is not None:
                instr.then_inc(self.sem_handles[inc[0]], inc[1])


def build_nc():
    nc = bass.Bass("TRN2", target_bir_lowering=False,
                   detect_race_conditions=False)

    def dram_in(name, shape):
        return nc.dram_tensor(name, list(shape), F32, kind="ExternalInput")

    xt_h = dram_in("xt", (NC_, 128, S))
    xtok_h = dram_in("xtok", (S, DM))
    wkT_h = dram_in("wkT", (NC_, 128, DM))
    wvT_h = dram_in("wvT", (NC_, 128, DM))
    wqL_h = dram_in("wqL", (H, 128, NC_, 128))
    woT_h = dram_in("woT", (NC_, 128, DM))
    sw1T_h = dram_in("sw1T", (DH, DH))
    sb1_h = dram_in("sb1c", (DH, 1))
    sw2_h = dram_in("sw2c", (DH, 1))
    sb2_h = dram_in("sb2r", (1, H))
    w1L_h = dram_in("w1L", (NF1, 128, NC_, 128))
    b1_h = dram_in("b1b", (128, NF1))
    w2L_h = dram_in("w2L", (NF1, 128, NF1, 128))
    b2_h = dram_in("b2b", (128, NF1))
    w3T_h = dram_in("w3T", (NF1, 128, DM))
    b3_h = dram_in("b3r", (1, DM))

    y_h = nc.dram_tensor("y", [S, DM], F32, kind="ExternalOutput")
    y1s_h = nc.dram_tensor("y1scr", [NT, 128, DM], F32, kind="Internal")
    w1scr = nc.dram_tensor("w1scr", [NF1, 128, NC_ * 128], BF16, kind="Internal")
    y1Tscr = nc.dram_tensor("y1Tscr", [QT, 128, NC_, 4, 128], BF16, kind="Internal")
    qTscr = nc.dram_tensor("qTscr", [H, 128, NT * 128], BF16, kind="Internal")
    w2scr = nc.dram_tensor("w2scr", [NF1, 128, NF1 * 128], BF16, kind="Internal")
    w3scr = nc.dram_tensor("w3scr", [4, 128, NF1, 512], BF16, kind="Internal")

    TOP = nc.sbuf_top
    S1 = TOP - 64 * 1024
    S2 = TOP - 128 * 1024
    S3 = S2 - 32 * 1024
    S4 = S3 - 40 * 1024

    def at(name, shape, dtype, off):
        return nc.alloc_sbuf_tensor_at(name, list(shape), dtype, offset=off)

    def sm(name, shape, dtype=F32):
        return nc.alloc_sbuf_tensor(name, list(shape), dtype)

    ident_f = sm("ident_f", [128, 128], F32)
    ident_b = sm("ident_b", [128, 128], BF16)
    jrow = sm("jrow", [128, 128], F32)
    ones_b = sm("ones_b", [1, 128], BF16)
    sw1_b = sm("sw1_b", [128, 128], BF16)
    sw2_b = sm("sw2_b", [128, 1], BF16)
    sb1_s = sm("sb1_s", [128, 1], F32)
    sb2_s = sm("sb2_s", [1, H], BF16)
    b1_s = sm("b1_s", [128, NF1], F32)
    b2_s = sm("b2_s", [128, NF1], F32)
    zcol = sm("zcol", [128, 1], F32)
    eps_c = sm("eps_c", [128, 1], F32)
    m_c = sm("m_c", [128, 1], F32)
    nm_c = sm("nm_c", [128, 1], F32)
    se_c = sm("se_c", [128, 1], F32)
    rc_c = sm("rc_c", [128, 1], F32)
    mean_c = sm("mean_c", [128, 2], F32)
    sq_c = sm("sq_c", [128, 2], F32)
    var_c = sm("var_c", [128, 2], F32)
    sd_c = sm("sd_c", [128, 2], F32)
    rs_c = sm("rs_c", [128, 2], F32)
    lo_c = sm("lo_c", [H, 1], F32)
    hi_c = sm("hi_c", [H, 1], F32)
    mid_c = sm("mid_c", [H, 1], F32)
    mids_c = sm("mids_c", [H, 1], F32)
    cnt_c = sm("cnt_c", [H, 1], F32)
    sel_c = sm("sel_c", [H, 1], F32)
    d0_c = sm("d0_c", [H, 1], F32)
    mx4 = sm("mx4", [128, 2, 4], F32)
    nm4 = sm("nm4", [128, 2, 4], F32)
    se4 = sm("se4", [128, 2, 4], F32)
    rc4 = sm("rc4", [128, 2, 4], F32)
    kvt_sb = sm("kvt_sb", [128, NC_ * H], F32)
    rpt = sm("rpt", [128, NC_, H], F32)


    xT_bf = at("xT_bf", [128, NC_, S], BF16, S2)
    woT_bf = at("woT_bf", [128, NC_, DM], BF16, S2)
    k_tok = at("k_tok", [128, NT, DM], BF16, S1)
    v_tok = at("v_tok", [128, NT, DM], BF16, S1)
    aT = at("aT", [128, H, S], BF16, S1)

    kvA = at("kvA", [H, S], F32, S3)
    kvB = at("kvB", [H, S], F32, S3 + 8 * 1024)
    kvC = at("kvC", [H, S], F32, S3 + 16 * 1024)
    kselT = at("kselT", [128, H, 128], BF16, S3 + 24 * 1024)
    vsel = at("vsel", [128, H, 128], BF16, S3 + 28 * 1024)
    qT_hb = at("qT_hb", [128, NT, 128], BF16, S4 + 8 * 1024)
    attn_b = at("attn_b", [128, 2, 4, 128], BF16, S4 + 12 * 1024)
    attnn_b = at("attnn_b", [128, 2, 4, 128], BF16, S4 + 14 * 1024)
    attnt_b = at("attnt_b", [128, 2, 4, 128], BF16, S4 + 16 * 1024)
    xtok_t = at("xtok_t", [128, 2, DM], F32, S3)
    u1_t = at("u1_t", [128, DM], F32, S3 + 16 * 1024)
    y1a_t = at("y1a_t", [128, DM], F32, S3 + 24 * 1024)
    y1b_t = at("y1b_t", [128, DM], F32, S4)
    y1q = at("y1q", [128, 4, DM], F32, S3)
    y1b16 = [at("y1b16a", [128, DM], BF16, S4 + 16 * 1024),
             at("y1b16b", [128, DM], BF16, S4 + 20 * 1024)]
    u1_t2 = at("u1b_t", [128, DM], F32, S4 + 24 * 1024)
    pstg = [at("pstg0", [128, 2048], BF16, S4 + 32 * 1024),
            at("pstg1", [128, 2048], BF16, S4 + 36 * 1024)]
    sqscr = at("sqscr", [128, DM], F32, S4 + 28 * 1024)

    wslab = [at("wslab0", [128, NC_, 512], BF16, S4),
             at("wslab1", [128, NC_, 512], BF16, S4 + 16 * 1024)]
    kT_h = at("kT_h", [128, NT, 128], BF16, S3)
    h1sT = at("h1sT", [128, NT, 128], BF16, S3 + 4 * 1024)
    mt_b = [at("mt_b0", [128, NC_, 128], BF16, S3 + 8 * 1024),
            at("mt_b1", [128, NC_, 128], BF16, S3 + 12 * 1024)]
    qld = [at("qld0", [128, NT, 128], BF16, S3 + 8 * 1024),
           at("qld1", [128, NT, 128], BF16, S3 + 12 * 1024)]
    ksel_tmp = at("ksel_tmp", [128, 128], BF16, S3 + 16 * 1024)
    wqsl = [at("wqsl0", [128, NC_, 128], BF16, S4),
            at("wqsl1", [128, NC_, 128], BF16, S4 + 4 * 1024)]
    w1sl = [at("w1sl0", [128, NC_, 128], BF16, S4),
            at("w1sl1", [128, NC_, 128], BF16, S4 + 4 * 1024)]
    w2sl = [at("w2sl0", [128, NF1, 128], BF16, S4 + 8 * 1024),
            at("w2sl1", [128, NF1, 128], BF16, S4 + 16 * 1024)]
    b3_s = at("b3_s", [1, DM], BF16, S4 + 24 * 1024)

    y1T_q = [at("y1T_qa", [128, NC_, QS], BF16, S2),
             at("y1T_qb", [128, NC_, QS], BF16, S2 + 16 * 1024)]
    h1T_q = at("h1T_q", [128, NF1, QS], BF16, S2 + 32 * 1024)
    h2T_q = at("h2T_q", [128, NF1, QS], BF16, S1)
    w3sl = [at("w3sl0", [128, NF1, 512], BF16, S1 + 32 * 1024),
            at("w3sl1", [128, NF1, 512], BF16, S2 + 32 * 1024)]


    pf = [nc.alloc_psum_tensor("pf%d" % i, [128, 512], F32) for i in range(4)]
    pb = [nc.alloc_psum_tensor("pb%d" % i, [128, 512], BF16) for i in range(3)]
    pkv = nc.alloc_psum_tensor("pkv", [128, 512], F32)

    nc.const_aps.register(F32, 0.0, zcol[:, :])
    p = Plan(nc)
    ps_free = {t.name: None for t in pf + pb + [pkv]}
    ps_ready = {}

    def mmgroup(psum_name, ops, deps=()):
        d = list(deps)
        if ps_free[psum_name] is not None:
            d.append(ps_free[psum_name])
        for fn in ops[:-1]:
            p.emit("T", fn, deps=tuple(d))
            d = ()
        tokn = p.chain("T", ops[-1], deps=tuple(d))
        ps_ready[psum_name] = tokn
        return tokn

    def evac(eng, fn, psum_name, deps=()):
        d = list(deps) + [ps_ready[psum_name]]
        tokn = p.chain(eng, fn, deps=d)
        ps_free[psum_name] = tokn
        return tokn

    def transp(pn_ap, pn_name, fn, deps=()):
        d = list(deps)
        if ps_free[pn_name] is not None:
            d.append(ps_free[pn_name])
        tokn = p.chain("T", fn, deps=d)
        ps_ready[pn_name] = tokn
        return tokn

    def gdma(dst, src, deps=(), sname="GD"):
        return p.emit("G", (lambda g, dst=dst, src=src: g.dma_start(dst, src)),
                      deps=deps, inc=(sname, 16))

    def qdma(dst, src, deps=(), sname="QD"):
        return p.emit("Q", (lambda q, dst=dst, src=src: q.dma_start(dst, src)),
                      deps=deps, inc=(sname, 16))

    def vo(fn, deps=()):
        return p.chain("V", fn, deps=deps)

    def so(fn, deps=()):
        return p.chain("S", fn, deps=deps)

    def gop(fn, deps=()):
        return p.chain("G", fn, deps=deps)

    for i in range(4):
        gdma(xT_bf[:, 4 * i:4 * i + 4, :],
             xt_h[4 * i:4 * i + 4, :, :].transpose([1, 0, 2]), sname="XT")
    XT_TOK = ("XT", p.cnt["XT"])
    gdma(sw1_b[:, :], sw1T_h[:, :], sname="SML")
    gdma(sw2_b[:, :], sw2_h[:, :], sname="SML")
    gdma(sb1_s[:, :], sb1_h[:, :], sname="SML")
    gdma(sb2_s[:, :], sb2_h[:, :], sname="SML")
    gdma(b1_s[:, :], b1_h[:, :], sname="SML")
    gdma(b2_s[:, :], b2_h[:, :], sname="SML")
    SML_TOK = ("SML", p.cnt["SML"])

    gop(lambda g: g.memset(ident_f[:, :], 0.0))
    gop(lambda g: g.memset(ident_b[:, :], 0.0))
    gop(lambda g: g.memset(zcol[:, :], 0.0))
    gop(lambda g: g.memset(eps_c[:, :], EPS))
    gop(lambda g: g.memset(ones_b[:, :], 1.0))
    gop(lambda g: g.affine_select(out=ident_f[:, :], in_=ident_f[:, :],
                                  compare_op=ALU.not_equal, fill=1.0, base=0,
                                  pattern=[[-1, 128]], channel_multiplier=1))
    gop(lambda g: g.affine_select(out=ident_b[:, :], in_=ident_b[:, :],
                                  compare_op=ALU.not_equal, fill=1.0, base=0,
                                  pattern=[[-1, 128]], channel_multiplier=1))
    SETUP_TOK = gop(lambda g: g.iota(jrow[:, :], pattern=[[1, 128]], base=1,
                                     channel_multiplier=0,
                                     allow_small_or_imprecise_dtypes=True))

    prep_units = []
    for f1 in range(NF1):
        prep_units.append(("w1", f1, 0))
    for f2 in range(NF1):
        prep_units.append(("w2", f2, 0))
        prep_units.append(("w2", f2, 1))
    for f3 in range(4):
        for i in range(8):
            prep_units.append(("w3", f3, i))
    prep_state = {"i": 0, "war": [None, None]}

    def prep_pump(n):
        for _ in range(n):
            i = prep_state["i"]
            if i >= len(prep_units):
                return
            prep_state["i"] = i + 1
            kind, a, b = prep_units[i]
            buf = i % 2
            st = pstg[buf]
            d = []
            if prep_state["war"][buf] is not None:
                d.append(prep_state["war"][buf])
            if kind == "w1":
                gdma(st[:, :], w1L_h[a].rearrange("p c m -> p (c m)"),
                     deps=tuple(d), sname="PG%d" % buf)
            elif kind == "w2":
                gdma(st[:, :],
                     w2L_h[a, :, 16 * b:16 * b + 16, :].rearrange("p c m -> p (c m)"),
                     deps=tuple(d), sname="PG%d" % buf)
            else:
                gdma(st[:, :].rearrange("p (c x) -> p c x", c=4),
                     w3T_h[4 * b:4 * b + 4, :, 512 * a:512 * (a + 1)]
                     .transpose([1, 0, 2]),
                     deps=tuple(d), sname="PG%d" % buf)
            gtok = ("PG%d" % buf, p.cnt["PG%d" % buf])
            if kind == "w1":
                qdma(w1scr[a], st[:, :], deps=[gtok], sname="PQ%d" % buf)
            elif kind == "w2":
                qdma(w2scr[a, :, 2048 * b:2048 * (b + 1)], st[:, :], deps=[gtok],
                     sname="PQ%d" % buf)
            else:
                qdma(w3scr[a, :, 4 * b:4 * b + 4, :],
                     st[:, :].rearrange("p (c x) -> p c x", c=4), deps=[gtok],
                     sname="PQ%d" % buf)
            prep_state["war"][buf] = ("PQ%d" % buf, p.cnt["PQ%d" % buf])

    slab_war = [None, None]
    slab_tok = [None, None]

    def proj_token_major(w_dram, dest, base_deps, s_only=False, after_fb=None):
        for fb in range(4):
            buf = fb % 2
            d = [SETUP_TOK] + list(base_deps)
            if slab_war[buf] is not None:
                d.append(slab_war[buf])
            for piece in range(2):
                gdma(wslab[buf][:, 8 * piece:8 * piece + 8, :],
                     w_dram[8 * piece:8 * piece + 8, :, 512 * fb:512 * (fb + 1)]
                     .transpose([1, 0, 2]),
                     deps=tuple(d), sname="WS%d" % buf)
                d = []
            slab_tok[buf] = ("WS%d" % buf, p.cnt["WS%d" % buf])
            for ts in range(NT):
                pn = pf[ts % 2]
                ops = []
                for dc in range(NC_):
                    def mk(t, dc=dc, ts=ts, pn=pn, buf=buf):
                        return t.matmul(pn[:, :],
                                        xT_bf[:, dc, 128 * ts:128 * (ts + 1)],
                                        wslab[buf][:, dc, :],
                                        start=(dc == 0), stop=(dc == NC_ - 1))
                    ops.append(mk)
                mmgroup(pn.name, ops, deps=[XT_TOK, slab_tok[buf]])
                eng = "S" if s_only else ("V" if ts % 2 == 0 else "S")
                if eng == "V":
                    def ev(e, ts=ts, fb=fb, pn=pn):
                        return e.tensor_copy(dest[:, ts, 512 * fb:512 * (fb + 1)],
                                             pn[:, :])
                else:
                    def ev(e, ts=ts, fb=fb, pn=pn):
                        return e.copy(dest[:, ts, 512 * fb:512 * (fb + 1)], pn[:, :])
                evac(eng, ev, pn.name)
            slab_war[buf] = p.tok("T")
            prep_pump(8)
            if after_fb is not None:
                after_fb(fb, p.tok("V"), p.tok("S"))

    kv_state = {"first": True}

    def kv_head_pipeline(h, vtok, stok):
        for ts in range(NT):
            pbn = pb[ts % 2]
            def tr(t, ts=ts, h=h, pbn=pbn):
                return t.transpose(pbn[:, 0:128],
                                   k_tok[:, ts, 128 * h:128 * (h + 1)],
                                   ident_b[:, :])
            transp(pbn, pbn.name, tr, deps=[SETUP_TOK, vtok, stok])
            evac("V", (lambda e, ts=ts, pbn=pbn:
                       e.tensor_copy(kT_h[:, ts, :], pbn[:, 0:128])), pbn.name)
        kT_tok = p.tok("V")
        for tb in range(4):
            pn = pf[2 + tb % 2]
            def mm(t, tb=tb, pn=pn):
                return t.matmul(pn[:, :], sw1_b[:, :], kT_h[:, 4 * tb:4 * tb + 4, :],
                                start=True, stop=True)
            mmgroup(pn.name, [mm], deps=[kT_tok, SML_TOK])
            def ev(e, tb=tb, pn=pn):
                return e.activation(h1sT[:, 4 * tb:4 * tb + 4, :], pn[:, :],
                                    AF.Relu, bias=sb1_s[:, :], scale=1.0)
            evac("S", ev, pn.name)
        h1_tok = p.tok("S")
        prep_pump(3)
        d = [h1_tok, SML_TOK]
        for c in range(NC_):
            first = kv_state["first"]
            kv_state["first"] = False
            def mk(t, c=c, h=h, first=first):
                return t.matmul(pkv[:, c * H + h:c * H + h + 1],
                                h1sT[:, c, :], sw2_b[:, :],
                                start=first, stop=False)
            p.emit("T", mk, deps=tuple(d))
            d = []

    def p1_after_fb(fb, vtok, stok):
        for h in range(4 * fb, 4 * fb + 4):
            kv_head_pipeline(h, vtok, stok)

    proj_token_major(wkT_h, k_tok, [], after_fb=p1_after_fb)
    for c in range(NC_):
        def mk(t, c=c):
            return t.matmul(pkv[:, H * c:H * (c + 1)], ones_b[:, :], sb2_s[0:1, :],
                            start=False, stop=(c == NC_ - 1))
        if c == NC_ - 1:
            tokn = p.chain("T", mk, deps=[SML_TOK])
            ps_ready[pkv.name] = tokn
        else:
            p.emit("T", mk, deps=[SML_TOK])
    evac("V", lambda e: e.tensor_copy(kvt_sb[:, :], pkv[:, 0:NC_ * H]), pkv.name)
    kvt_tok = p.tok("V")
    for c in range(NC_):
        pn = pf[c % 2]
        def tr(t, c=c, pn=pn):
            return t.transpose(pn[0:16, 0:128], kvt_sb[:, H * c:H * (c + 1)],
                               ident_f[:, :])
        transp(pn, pn.name, tr, deps=[kvt_tok])
        evac("V", (lambda e, c=c, pn=pn:
                   e.tensor_copy(kvA[:, 128 * c:128 * (c + 1)], pn[0:16, 0:128])),
             pn.name)

    wq_war = [None, None]
    qstg_war = None
    for h in range(H):
        buf = h % 2
        d = [SETUP_TOK, slab_war[0], slab_war[1]]
        if wq_war[buf] is not None:
            d.append(wq_war[buf])
        gdma(wqsl[buf][:, :, :], wqL_h[h], deps=tuple(d), sname="WQ%d" % buf)
        wq_tok = ("WQ%d" % buf, p.cnt["WQ%d" % buf])
        for tb in range(4):
            pn = pf[tb % 2]
            ops = []
            for dc in range(NC_):
                def mk(t, dc=dc, tb=tb, pn=pn, buf=buf):
                    return t.matmul(pn[:, :], wqsl[buf][:, dc, :],
                                    xT_bf[:, dc, 512 * tb:512 * (tb + 1)],
                                    start=(dc == 0), stop=(dc == NC_ - 1))
                ops.append(mk)
            mmgroup(pn.name, ops, deps=[XT_TOK, wq_tok])
            dd = [qstg_war] if (qstg_war is not None and tb == 0) else []
            def ev(e, tb=tb, pn=pn):
                return e.activation(qT_hb[:, 4 * tb:4 * tb + 4, :], pn[:, :],
                                    AF.Copy, bias=0.0, scale=SCALE)
            evac("S", ev, pn.name, deps=dd + [SML_TOK])
        wq_war[buf] = p.tok("T")
        q_stok = p.tok("S")
        qdma(qTscr[h], qT_hb[:, :, :].rearrange("p c m -> p (c m)"),
             deps=[q_stok], sname="QS")
        qstg_war = ("QS", p.cnt["QS"])
    QS_ALL = ("QS", p.cnt["QS"])

    vo(lambda v: v.tensor_reduce(lo_c[:, :], kvA[:, :], axis=AX.X, op=ALU.min))
    vo(lambda v: v.tensor_reduce(hi_c[:, :], kvA[:, :], axis=AX.X, op=ALU.max))
    vo(lambda v: v.tensor_scalar_add(lo_c[:, :], lo_c[:, :], -1.0))
    vo(lambda v: v.tensor_scalar_add(hi_c[:, :], hi_c[:, :], 1.0))
    for it in range(NBIN):
        vo(lambda v: v.tensor_add(mid_c[:, :], lo_c[:, :], hi_c[:, :]))
        vo(lambda v: v.tensor_scalar_mul(mids_c[:, :], mid_c[:, :], 0.5))
        vo(lambda v: v.tensor_scalar(kvB[:, :], kvA[:, :], mids_c[:, :], None,
                                     op0=ALU.is_ge, op1=ALU.add,
                                     accum_out=cnt_c[:, :]))
        vo(lambda v: v.tensor_scalar(sel_c[:, :], cnt_c[:, :], float(KSEL), None,
                                     op0=ALU.is_ge))
        vo(lambda v: v.tensor_sub(d0_c[:, :], mids_c[:, :], lo_c[:, :]))
        vo(lambda v: v.scalar_tensor_tensor(lo_c[:, :], d0_c[:, :], sel_c[:, :],
                                            lo_c[:, :], op0=ALU.mult,
                                            op1=ALU.add))
        vo(lambda v: v.tensor_sub(d0_c[:, :], hi_c[:, :], mids_c[:, :]))
        vo(lambda v: v.scalar_tensor_tensor(hi_c[:, :], d0_c[:, :], sel_c[:, :],
                                            mids_c[:, :], op0=ALU.mult,
                                            op1=ALU.add))
    vo(lambda v: v.tensor_scalar(kvB[:, :], kvA[:, :], lo_c[:, :], None,
                                 op0=ALU.is_ge))
    vo(lambda v: v.tensor_tensor_scan(kvC[:, :], kvB[:, :], kvB[:, :], 0.0,
                                      op0=ALU.add, op1=ALU.bypass))
    RP_TOK = vo(lambda v: v.tensor_mul(kvA[:, :], kvC[:, :], kvB[:, :]))
    for c in range(NC_):
        pn = pf[2 + c % 2]
        def tr(t, c=c, pn=pn):
            return t.transpose(pn[0:128, 0:16], kvA[:, 128 * c:128 * (c + 1)],
                               ident_f[0:16, 0:16])
        transp(pn, pn.name, tr, deps=[RP_TOK])
        evac("V", (lambda e, c=c, pn=pn:
                   e.tensor_copy(rpt[:, c, :], pn[0:128, 0:16])), pn.name)
    RPT_TOK = p.tok("V")

    gather_war = [None, None]

    def gather_mt(h, extra_dep):
        buf = h % 2
        d = [RPT_TOK, extra_dep]
        if gather_war[buf] is not None:
            d.append(gather_war[buf])
        vo((lambda v, h=h, buf=buf: v.tensor_tensor(
            mt_b[buf][:, :, :],
            jrow[:, :].unsqueeze(1).to_broadcast([128, NC_, 128]),
            rpt[:, :, h:h + 1].to_broadcast([128, NC_, 128]),
            op=ALU.is_equal)), deps=d)
        return p.tok("V")

    def gather_mm(dest_sel, src_tok, h, mt_tok, src_deps, transpose_out=False):
        buf = h % 2
        pn = pf[3]
        ops = []
        for c in range(NC_):
            def mk(t, c=c, h=h, pn=pn, buf=buf):
                return t.matmul(pn[:, 0:128], mt_b[buf][:, c, :],
                                src_tok[:, c, 128 * h:128 * (h + 1)],
                                start=(c == 0), stop=(c == NC_ - 1))
            ops.append(mk)
        mmgroup(pn.name, ops, deps=[mt_tok] + list(src_deps))
        gather_war[buf] = p.tok("T")
        if not transpose_out:
            evac("V", (lambda e, h=h: e.tensor_copy(dest_sel[:, h, :],
                                                    pn[:, 0:128])), pn.name)
        else:
            ev_tok = evac("V", (lambda e: e.tensor_copy(ksel_tmp[:, :],
                                                        pn[:, 0:128])), pn.name)
            pbn = pb[0]
            def tr(t, pbn=pbn):
                return t.transpose(pbn[:, 0:128], ksel_tmp[:, :], ident_b[:, :])
            transp(pbn, pbn.name, tr, deps=[ev_tok])
            evac("V", (lambda e, h=h: e.tensor_copy(dest_sel[:, h, :],
                                                    pbn[:, 0:128])), pbn.name)

    def gather_batch(dest_sel, src_tok, heads, extra_dep, src_deps,
                     transpose_out=False):
        mt_toks = {}
        pending = []
        for h in heads:
            mt_toks[h] = gather_mt(h, extra_dep)
            pending.append(h)
            if len(pending) >= 2:
                h0 = pending.pop(0)
                gather_mm(dest_sel, src_tok, h0, mt_toks[h0], src_deps,
                          transpose_out)
        for h0 in pending:
            gather_mm(dest_sel, src_tok, h0, mt_toks[h0], src_deps, transpose_out)


    KG_V = p.tok("V")
    KG_S = p.tok("S")
    gather_batch(kselT, k_tok, range(H), SETUP_TOK, [KG_V, KG_S],
                 transpose_out=True)
    KSEL_DONE = p.tok("V")

    def p3_after_fb(fb, vtok, stok):
        gather_batch(vsel, v_tok, range(4 * fb, 4 * fb + 4), SETUP_TOK,
                     [vtok, stok])

    proj_token_major(wvT_h, v_tok, [p.tok("T"), QS_ALL], s_only=True,
                     after_fb=p3_after_fb)
    VSEL_DONE = p.tok("V")

    P5_T0 = p.tok("T")
    qld_war = [None, None]
    wo_pieces = [0]
    for h in range(H):
        buf = h % 2
        d = [QS_ALL]
        if qld_war[buf] is not None:
            d.append(qld_war[buf])
        else:
            d.append(P5_T0)
        gdma(qld[buf][:, :, :].rearrange("p c m -> p (c m)"), qTscr[h],
             deps=tuple(d), sname="QL%d" % buf)
        q_tok = ("QL%d" % buf, p.cnt["QL%d" % buf])
        qT_h = qld[buf]
        prep_pump(2)
        def attn_scores(qsb):
            pn = pf[2 + qsb % 2]
            ops = []
            for j in range(4):
                def mk(t, j=j, qsb=qsb, h=h, pn=pn, qT_h=qT_h):
                    return t.matmul(pn[:, 128 * j:128 * (j + 1)],
                                    qT_h[:, 4 * qsb + j, :], kselT[:, h, :],
                                    start=(j == 0), stop=(j == 3))
                ops.append(mk)
            mmgroup(pn.name, ops, deps=[q_tok, KSEL_DONE])
            return pn

        def attn_maxnm(qsb, pn):
            pi = qsb % 2
            return vo((lambda v, pn=pn, pi=pi: v.reduce_max(
                nm4[:, pi, :], pn[:, :].rearrange("p (j x) -> p j x", j=4),
                axis=AX.X, negate=True)), deps=[ps_ready[pn.name]])

        def attn_exp(qsb, pn, nm_tok):
            pi = qsb % 2
            ex_tok = None
            for j in range(4):
                def ex(e, j=j, pn=pn, pi=pi):
                    return e.activation(attn_b[:, pi, j, :],
                                        pn[:, 128 * j:128 * (j + 1)],
                                        AF.Exp, bias=nm4[:, pi, j:j + 1],
                                        scale=1.0, accum_out=se4[:, pi, j:j + 1])
                ex_tok = so(ex, deps=[nm_tok] if j == 0 else [])
            ps_free[pn.name] = ex_tok
            return ex_tok

        def attn_norm(qsb, ex_tok):
            pi = qsb % 2
            vo((lambda v, pi=pi: v.reciprocal(rc4[:, pi, :], se4[:, pi, :])),
               deps=[ex_tok])
            for j in range(4):
                vo((lambda v, j=j, pi=pi: v.tensor_scalar_mul(
                    attnn_b[:, pi, j, :], attn_b[:, pi, j, :],
                    rc4[:, pi, j:j + 1])))
            return p.tok("V")

        def attn_trcopy(qsb, an_tok, banks):
            pi = qsb % 2
            at_tok = None
            for j in range(4):
                pbn = banks[j % 2]
                def tr(t, j=j, pbn=pbn, pi=pi):
                    return t.transpose(pbn[:, 128 * (j // 2):128 * (j // 2 + 1)],
                                       attnn_b[:, pi, j, :], ident_b[:, :])
                d = [an_tok]
                if ps_free[pbn.name] is not None:
                    d.append(ps_free[pbn.name])
                ttok = p.chain("T", tr, deps=d)
                ps_ready[pbn.name] = ttok
                def cpy(e, j=j, pbn=pbn, pi=pi):
                    return e.copy(attnt_b[:, pi, j, :],
                                  pbn[:, 128 * (j // 2):128 * (j // 2 + 1)])
                at_tok = so(cpy, deps=[ttok])
                ps_free[pbn.name] = at_tok
            return at_tok

        def attn_av(qsb, at_tok):
            pi = qsb % 2
            pn2 = pf[qsb % 2]
            ops2 = []
            for j in range(4):
                def av(t, j=j, h=h, pn2=pn2, pi=pi):
                    return t.matmul(pn2[:, 128 * j:128 * (j + 1)], vsel[:, h, :],
                                    attnt_b[:, pi, j, :], start=(j == 0),
                                    stop=(j == 3))
                ops2.append(av)
            mmgroup(pn2.name, ops2, deps=[at_tok, VSEL_DONE])
            evac("V", (lambda e, h=h, qsb=qsb, pn2=pn2:
                       e.tensor_copy(aT[:, h, 512 * qsb:512 * (qsb + 1)],
                                     pn2[:, :])), pn2.name)

        for pr in range(2):
            A, B = 2 * pr, 2 * pr + 1
            pnA = attn_scores(A)
            pnB = attn_scores(B)
            nmA = attn_maxnm(A, pnA)
            exA = attn_exp(A, pnA, nmA)
            nmB = attn_maxnm(B, pnB)
            rcA = attn_norm(A, exA)
            exB = attn_exp(B, pnB, nmB)
            atA = attn_trcopy(A, rcA, [pb[0], pb[1]])
            rcB = attn_norm(B, exB)
            atB = attn_trcopy(B, rcB, [pb[2], pb[1]])
            attn_av(A, atA)
            attn_av(B, atB)
        qld_war[buf] = p.tok("T")
    AT_DONE = p.tok("V")

    prep_pump(len(prep_units))
    PREP_DONE0 = ("PQ0", p.cnt["PQ0"])
    PREP_DONE1 = ("PQ1", p.cnt["PQ1"])
    T_AFTER_Q = p.tok("T")
    for i in range(4):
        gdma(woT_bf[:, 4 * i:4 * i + 4, :],
             woT_h[4 * i:4 * i + 4, :, :].transpose([1, 0, 2]),
             deps=([P5_T0] if i == 0 else ()), sname="WO")
    WO_TOK = ("WO", p.cnt["WO"])
    xt_war = [None, None]
    y1_war = [None, None]
    stg_war = [[PREP_DONE0, PREP_DONE1], [PREP_DONE0, PREP_DONE1]]
    pend_y1T = []
    for ts in range(NT):
        buf = ts % 2
        u1 = u1_t if buf == 0 else u1_t2
        d = [AT_DONE]
        if xt_war[buf] is not None:
            d.append(xt_war[buf])
        qdma(xtok_t[:, buf, :], xtok_h[128 * ts:128 * (ts + 1), :], deps=tuple(d),
             sname="XK%d" % buf)
        xk_tok = ("XK%d" % buf, p.cnt["XK%d" % buf])
        for fb in range(4):
            pn = pf[fb % 4]
            ops = []
            for hh in range(H):
                def mk(t, hh=hh, ts=ts, fb=fb, pn=pn):
                    return t.matmul(pn[:, :], aT[:, hh, 128 * ts:128 * (ts + 1)],
                                    woT_bf[:, hh, 512 * fb:512 * (fb + 1)],
                                    start=(hh == 0), stop=(hh == H - 1))
                ops.append(mk)
            mmgroup(pn.name, ops, deps=[WO_TOK, AT_DONE])
            def ev(e, fb=fb, pn=pn, buf=buf, u1=u1):
                return e.tensor_tensor(u1[:, 512 * fb:512 * (fb + 1)], pn[:, :],
                                       xtok_t[:, buf, 512 * fb:512 * (fb + 1)],
                                       op=ALU.add)
            tokn = evac("V", ev, pn.name, deps=[xk_tok])
            if fb == 3:
                xt_war[buf] = tokn
        if pend_y1T:
            pend_y1T.pop(0)()
        U1_TOK = p.tok("V")
        y1t = y1a_t if buf == 0 else y1b_t
        ysem = "YW%d" % buf
        d_sq = [U1_TOK, SML_TOK]
        if y1_war[buf] is not None:
            d_sq.append(y1_war[buf])
        vo((lambda v, u1=u1, buf=buf: v.reduce_sum(mean_c[:, buf:buf + 1],
                                                   u1[:, :], axis=AX.X)))
        MN_TOK = vo((lambda v, buf=buf: v.tensor_scalar_mul(
            mean_c[:, buf:buf + 1], mean_c[:, buf:buf + 1], 1.0 / DM)))
        SQ_TOK = so((lambda s, y1t=y1t, u1=u1, buf=buf: s.activation(
            y1t[:, :], u1[:, :], AF.Square, bias=zcol[:, :], scale=1.0,
            accum_out=sq_c[:, buf:buf + 1])), deps=d_sq)
        vo((lambda v, buf=buf: v.tensor_scalar_mul(
            var_c[:, buf:buf + 1], sq_c[:, buf:buf + 1], 1.0 / DM)),
           deps=[SQ_TOK])
        vo((lambda v, buf=buf: v.tensor_mul(sd_c[:, buf:buf + 1],
                                            mean_c[:, buf:buf + 1],
                                            mean_c[:, buf:buf + 1])))
        VV_TOK = vo((lambda v, buf=buf: v.tensor_sub(var_c[:, buf:buf + 1],
                                                     var_c[:, buf:buf + 1],
                                                     sd_c[:, buf:buf + 1])))
        SD_TOK = so((lambda s, buf=buf: s.activation(
            sd_c[:, buf:buf + 1], var_c[:, buf:buf + 1], AF.Sqrt,
            bias=eps_c[:, :], scale=1.0)), deps=[VV_TOK])
        RS_TOK = vo((lambda v, buf=buf: v.reciprocal(rs_c[:, buf:buf + 1],
                                                     sd_c[:, buf:buf + 1])),
                    deps=[SD_TOK])
        Y1_TOK = vo((lambda v, y1t=y1t, u1=u1, buf=buf: v.tensor_scalar(
            y1t[:, :], u1[:, :], mean_c[:, buf:buf + 1], rs_c[:, buf:buf + 1],
            op0=ALU.subtract, op1=ALU.mult)), deps=[RS_TOK, MN_TOK])
        qdma(y1s_h[ts, :, :], y1t[:, :], deps=[Y1_TOK], sname=ysem)
        y1_war[buf] = (ysem, p.cnt[ysem])
        yb = y1b16[buf]
        Y1B_TOK = so((lambda s, y1t=y1t, yb=yb: s.copy(yb[:, :], y1t[:, :])),
                     deps=[Y1_TOK])

        def do_y1T(ts=ts, buf=buf, yb=yb, Y1B_TOK=Y1B_TOK):
            stg = pstg[buf]
            ev_toks = []
            for dc in range(NC_):
                pbn = pb[dc % 3]
                def tr(t, dc=dc, pbn=pbn, yb=yb):
                    return t.transpose(pbn[:, 0:128],
                                       yb[:, 128 * dc:128 * (dc + 1)],
                                       ident_b[:, :])
                transp(pbn, pbn.name, tr, deps=[Y1B_TOK, SETUP_TOK])
                eng = "V" if dc % 2 == 0 else "S"
                dd = list(stg_war[buf]) if dc < 2 else []
                if eng == "V":
                    def ev2(e, dc=dc, pbn=pbn, stg=stg):
                        return e.tensor_copy(stg[:, 128 * dc:128 * (dc + 1)],
                                             pbn[:, 0:128])
                else:
                    def ev2(e, dc=dc, pbn=pbn, stg=stg):
                        return e.copy(stg[:, 128 * dc:128 * (dc + 1)],
                                      pbn[:, 0:128])
                ev_toks.append(evac(eng, ev2, pbn.name, deps=dd))
            qdma(y1Tscr[ts // 4, :, :, ts % 4, :],
                 stg[:, :].rearrange("p (c x) -> p c x", c=NC_),
                 deps=[ev_toks[-1], ev_toks[-2]], sname="YT%d" % buf)
            stg_war[buf] = [("YT%d" % buf, p.cnt["YT%d" % buf])]
        pend_y1T.append(do_y1T)
    for fn_ in pend_y1T:
        fn_()
    pend_y1T = []
    Y1W0 = ("YW0", p.cnt["YW0"])
    Y1W1 = ("YW1", p.cnt["YW1"])
    YT0 = ("YT0", p.cnt["YT0"])
    YT1 = ("YT1", p.cnt["YT1"])

    P7_T0 = p.tok("T")
    gdma(b3_s[:, :], b3_h[:, :], deps=[P7_T0], sname="SMX")
    B3_TOK = ("SMX", p.cnt["SMX"])
    w1_war = [None, None]
    w2_war = [None, None]
    w3_war = [None, None]
    y1tq_war = [None, None]
    ln2_prev = None
    yo_prev = []
    def yq_prefetch(qq):
        bufy = qq % 2
        d = [YT0, YT1]
        if y1tq_war[bufy] is not None:
            d.append(y1tq_war[bufy])
        qdma(y1T_q[bufy][:, :, :], y1Tscr[qq].rearrange("p c s x -> p c (s x)"),
             deps=tuple(d), sname="YQ%d" % bufy)
        return ("YQ%d" % bufy, p.cnt["YQ%d" % bufy])

    yq_next = yq_prefetch(0)
    for qq in range(QT):
        bufy = qq % 2
        yq_tok = yq_next
        for f1 in range(NF1):
            buf = f1 % 2
            if qq > 0 and f1 < 2:
                w1_tok = w1_pre[f1]
            else:
                d = [PREP_DONE0, PREP_DONE1]
                d.append(w1_war[buf] if w1_war[buf] is not None else Y1W0)
                if w1_war[buf] is None:
                    d.append(Y1W1)
                gdma(w1sl[buf][:, :, :].rearrange("p c m -> p (c m)"), w1scr[f1],
                     deps=tuple(d), sname="W1%d" % buf)
                w1_tok = ("W1%d" % buf, p.cnt["W1%d" % buf])
            pn = pf[f1 % 2]
            ops = []
            for dc in range(NC_):
                def mk(t, dc=dc, pn=pn, buf=buf, bufy=bufy):
                    return t.matmul(pn[:, :], w1sl[buf][:, dc, :],
                                    y1T_q[bufy][:, dc, :], start=(dc == 0),
                                    stop=(dc == NC_ - 1))
                ops.append(mk)
            mmgroup(pn.name, ops, deps=[w1_tok, yq_tok])
            w1_war[buf] = p.tok("T")
            def ev(e, f1=f1, pn=pn):
                return e.activation(h1T_q[:, f1, :], pn[:, :], AF.Relu,
                                    bias=b1_s[:, f1:f1 + 1], scale=1.0)
            evac("S", ev, pn.name, deps=[SML_TOK])
        y1tq_war[bufy] = p.tok("T")
        H1_TOK = p.tok("S")
        for f2 in range(NF1):
            buf = f2 % 2
            d = [PREP_DONE0, PREP_DONE1]
            d.append(w2_war[buf] if w2_war[buf] is not None else P7_T0)
            qdma(w2sl[buf][:, :, :].rearrange("p c m -> p (c m)"), w2scr[f2],
                 deps=tuple(d), sname="W2%d" % buf)
            w2_tok = ("W2%d" % buf, p.cnt["W2%d" % buf])
            pn = pf[f2 % 2]
            ops = []
            for fc in range(NF1):
                def mk(t, fc=fc, pn=pn, buf=buf):
                    return t.matmul(pn[:, :], w2sl[buf][:, fc, :],
                                    h1T_q[:, fc, :], start=(fc == 0),
                                    stop=(fc == NF1 - 1))
                ops.append(mk)
            mmgroup(pn.name, ops, deps=[w2_tok, H1_TOK])
            w2_war[buf] = p.tok("T")
            def ev(e, f2=f2, pn=pn):
                return e.activation(h2T_q[:, f2, :], pn[:, :], AF.Relu,
                                    bias=b2_s[:, f2:f2 + 1], scale=1.0)
            evac("S", ev, pn.name, deps=[SML_TOK])
        H2_TOK = p.tok("S")
        if qq + 1 < QT:
            yq_next = yq_prefetch(qq + 1)
            w1_pre = []
            for f1 in range(2):
                d = [PREP_DONE0, PREP_DONE1, w1_war[f1 % 2]]
                gdma(w1sl[f1 % 2][:, :, :].rearrange("p c m -> p (c m)"),
                     w1scr[f1], deps=tuple(d), sname="W1%d" % (f1 % 2))
                w1_pre.append(("W1%d" % (f1 % 2), p.cnt["W1%d" % (f1 % 2)]))
        d = [Y1W0, Y1W1] + yo_prev
        if ln2_prev is not None:
            d.append(ln2_prev)
        qdma(y1q[:, :, :], y1s_h[4 * qq:4 * qq + 4, :, :].transpose([1, 0, 2]),
             deps=tuple(d), sname="Y1R")
        y1r_tok = ("Y1R", p.cnt["Y1R"])
        W2_T = p.tok("T")
        ev_last = [None] * 4
        for f3 in range(4):
            wb = w3sl[f3 % 2]
            d = [PREP_DONE0, PREP_DONE1]
            if f3 % 2 == 1:
                d.append(W2_T)
            d.append(w3_war[f3 % 2] if w3_war[f3 % 2] is not None else T_AFTER_Q)
            gdma(wb[:, :, :], w3scr[f3], deps=tuple(d), sname="W3S%d" % (f3 % 2))
            w3_tok = ("W3S%d" % (f3 % 2), p.cnt["W3S%d" % (f3 % 2)])
            for ts in range(4):
                pn = pf[2 + ts % 2]
                ops = [lambda t, f3=f3, pn=pn: t.matmul(
                    pn[:, :], ones_b[:, :], b3_s[0:1, 512 * f3:512 * (f3 + 1)],
                    start=True, stop=False)]
                for fc in range(NF1):
                    def mk(t, fc=fc, ts=ts, pn=pn, wb=wb):
                        return t.matmul(pn[:, :],
                                        h2T_q[:, fc, 128 * ts:128 * (ts + 1)],
                                        wb[:, fc, :], start=False,
                                        stop=(fc == NF1 - 1))
                    ops.append(mk)
                mmgroup(pn.name, ops, deps=[w3_tok, B3_TOK, H2_TOK])
                def ev(e, ts=ts, f3=f3, pn=pn):
                    return e.tensor_tensor(y1q[:, ts, 512 * f3:512 * (f3 + 1)],
                                           pn[:, :],
                                           y1q[:, ts, 512 * f3:512 * (f3 + 1)],
                                           op=ALU.add)
                ev_last[ts] = evac("V", ev, pn.name, deps=[y1r_tok])
            w3_war[f3 % 2] = p.tok("T")
        for ts in range(4):
            vo((lambda v, ts=ts: v.reduce_sum(mean_c[:, 0:1], y1q[:, ts, :],
                                              axis=AX.X)), deps=[ev_last[ts]])
            MN2 = vo(lambda v: v.tensor_scalar_mul(mean_c[:, 0:1], mean_c[:, 0:1],
                                                   1.0 / DM))
            d_sq = [ev_last[ts], SML_TOK]
            if qq == 0 and ts == 0:
                d_sq += [YT0, YT1]
            SQ2 = so((lambda s, ts=ts: s.activation(
                sqscr[:, :], y1q[:, ts, :], AF.Square, bias=zcol[:, :], scale=1.0,
                accum_out=sq_c[:, 0:1])), deps=d_sq)
            vo(lambda v: v.tensor_scalar_mul(var_c[:, 0:1], sq_c[:, 0:1], 1.0 / DM),
               deps=[SQ2])
            vo(lambda v: v.tensor_mul(sd_c[:, 0:1], mean_c[:, 0:1], mean_c[:, 0:1]),
               deps=[MN2])
            VV2 = vo(lambda v: v.tensor_sub(var_c[:, 0:1], var_c[:, 0:1], sd_c[:, 0:1]))
            SD2 = so(lambda s: s.activation(sd_c[:, 0:1], var_c[:, 0:1], AF.Sqrt,
                                            bias=eps_c[:, :], scale=1.0),
                     deps=[VV2])
            RS2 = vo(lambda v: v.reciprocal(rs_c[:, 0:1], sd_c[:, 0:1]), deps=[SD2])
            Y2 = vo((lambda v, ts=ts: v.tensor_scalar(
                y1q[:, ts, :], y1q[:, ts, :], mean_c[:, 0:1], rs_c[:, 0:1],
                op0=ALU.subtract, op1=ALU.mult)), deps=[RS2])
            qdma(y_h[128 * (4 * qq + ts):128 * (4 * qq + ts + 1), :],
                 y1q[:, ts, :], deps=[Y2], sname="YO%d" % (ts % 2))
        ln2_prev = p.tok("V")
        yo_prev = [("YO0", p.cnt["YO0"]), ("YO1", p.cnt["YO1"])]

    p.emit("Q", lambda q: None, deps=[("YO0", p.cnt["YO0"]),
                                      ("YO1", p.cnt["YO1"])])

    with nc.Block() as block:
        @block.tensor
        def _(t):
            p.replay("T", t)

        @block.vector
        def _(v):
            p.replay("V", v)

        @block.scalar
        def _(s):
            p.replay("S", s)

        @block.sync
        def _(q):
            p.replay("Q", q)

        @block.gpsimd
        def _(g):
            p.replay("G", g)

    return nc


_NC_CACHE = None


def _get_nc():
    global _NC_CACHE
    if _NC_CACHE is None:
        _NC_CACHE = build_nc()
    return _NC_CACHE


def kernel(L0_attn_x, L0_attn_wq, L0_attn_wk, L0_attn_wv, L0_attn_wo,
           L0_attn_sw1, L0_attn_sb1, L0_attn_sw2, L0_attn_sb2,
           L0_ffn_W1, L0_ffn_b1, L0_ffn_W2, L0_ffn_b2, L0_ffn_W3, L0_ffn_b3):
    x = np.asarray(L0_attn_x, dtype=np.float32)
    B = x.shape[0]
    C = np.ascontiguousarray

    wq = np.asarray(L0_attn_wq, np.float32)
    wk = np.asarray(L0_attn_wk, np.float32)
    wv = np.asarray(L0_attn_wv, np.float32)
    wo = np.asarray(L0_attn_wo, np.float32)
    W1 = np.asarray(L0_ffn_W1, np.float32)
    W2 = np.asarray(L0_ffn_W2, np.float32)
    W3 = np.asarray(L0_ffn_W3, np.float32)

    shared = {
        "wkT": C(wk.T.reshape(NC_, 128, DM)),
        "wvT": C(wv.T.reshape(NC_, 128, DM)),
        "woT": C(wo.T.reshape(NC_, 128, DM)),
        "wqL": C(wq.reshape(H, 128, NC_, 128).transpose(0, 3, 2, 1)),
        "sw1T": C(np.asarray(L0_attn_sw1, np.float32).T),
        "sb1c": C(np.asarray(L0_attn_sb1, np.float32).reshape(DH, 1)),
        "sw2c": C(np.asarray(L0_attn_sw2, np.float32).reshape(1, DH).T),
        "sb2r": C(np.broadcast_to(
            np.asarray(L0_attn_sb2, np.float32).reshape(1, 1), (1, H))),
        "w1L": C(W1.reshape(NF1, 128, NC_, 128).transpose(0, 3, 2, 1)),
        "b1b": C(np.asarray(L0_ffn_b1, np.float32).reshape(NF1, 128).T),
        "w2L": C(W2.reshape(NF1, 128, NF1, 128).transpose(0, 3, 2, 1)),
        "b2b": C(np.asarray(L0_ffn_b2, np.float32).reshape(NF1, 128).T),
        "w3T": C(W3.T.reshape(NF1, 128, DM)),
        "b3r": C(np.asarray(L0_ffn_b3, np.float32).reshape(1, DM)),
    }
    in_maps = []
    for b in range(B):
        m = dict(shared)
        m["xt"] = C(x[b].T.reshape(NC_, 128, S))
        m["xtok"] = C(x[b])
        in_maps.append(m)

    nc = _get_nc()
    res = run_bass_kernel_spmd(nc, in_maps, core_ids=list(range(B)))
    return np.stack([r["y"] for r in res.results], axis=0)


# version 23
# speedup vs baseline: 1.0367x; 1.0055x over previous
import math
import numpy as np
import concourse.bass as bass
import concourse.mybir as mybir
from concourse.bass_utils import run_bass_kernel_spmd

F32 = mybir.dt.float32
BF16 = mybir.dt.bfloat16
AF = mybir.ActivationFunctionType
ALU = mybir.AluOpType
AX = mybir.AxisListType

S = 2048
DM = 2048
H = 16
DH = 128
KSEL = 128
FF = 4096
NT = S // 128
NC_ = DM // 128
NF1 = FF // 128
SCALE = 1.0 / math.sqrt(DH)
EPS = 1e-5
NBIN = 48
QT = 4
QS = S // QT


class Plan:
    ENGS = ("T", "V", "S", "Q", "G")

    def __init__(self, nc):
        self.nc = nc
        self.items = {e: [] for e in self.ENGS}
        self.sem_handles = {}
        self.cnt = {}

    def sem(self, name):
        if name not in self.sem_handles:
            self.sem_handles[name] = self.nc.alloc_semaphore(name)
            self.cnt[name] = 0
        return self.sem_handles[name]

    def emit(self, eng, fn, deps=(), inc=None):
        tok = None
        if inc is not None:
            self.sem(inc[0])
            self.cnt[inc[0]] += inc[1]
            tok = (inc[0], self.cnt[inc[0]])
        self.items[eng].append((tuple(deps), fn, inc))
        return tok

    def chain(self, eng, fn, deps=()):
        cn = "c" + eng
        self.sem(cn)
        prev = self.cnt[cn]
        d = list(deps)
        if prev > 0:
            d.append((cn, prev))
        return self.emit(eng, fn, deps=d, inc=(cn, 1))

    def tok(self, eng):
        cn = "c" + eng
        self.sem(cn)
        return (cn, self.cnt[cn])

    def replay(self, eng_name, engine):
        watermark = {}
        for deps, fn, inc in self.items[eng_name]:
            for sname, val in deps:
                if val > 0 and watermark.get(sname, 0) < val:
                    engine.wait_ge(self.sem_handles[sname], val)
                    watermark[sname] = val
            instr = fn(engine)
            if inc is not None:
                instr.then_inc(self.sem_handles[inc[0]], inc[1])


def build_nc():
    nc = bass.Bass("TRN2", target_bir_lowering=False,
                   detect_race_conditions=False)

    def dram_in(name, shape):
        return nc.dram_tensor(name, list(shape), F32, kind="ExternalInput")

    xt_h = dram_in("xt", (NC_, 128, S))
    xtok_h = dram_in("xtok", (S, DM))
    wkT_h = dram_in("wkT", (NC_, 128, DM))
    wvT_h = dram_in("wvT", (NC_, 128, DM))
    wqL_h = dram_in("wqL", (H, 128, NC_, 128))
    woT_h = dram_in("woT", (NC_, 128, DM))
    sw1T_h = dram_in("sw1T", (DH, DH))
    sb1_h = dram_in("sb1c", (DH, 1))
    sw2_h = dram_in("sw2c", (DH, 1))
    sb2_h = dram_in("sb2r", (1, H))
    w1L_h = dram_in("w1L", (NF1, 128, NC_, 128))
    b1_h = dram_in("b1b", (128, NF1))
    w2L_h = dram_in("w2L", (NF1, 128, NF1, 128))
    b2_h = dram_in("b2b", (128, NF1))
    w3T_h = dram_in("w3T", (NF1, 128, DM))
    b3_h = dram_in("b3r", (1, DM))

    y_h = nc.dram_tensor("y", [S, DM], F32, kind="ExternalOutput")
    y1s_h = nc.dram_tensor("y1scr", [NT, 128, DM], F32, kind="Internal")
    w1scr = nc.dram_tensor("w1scr", [NF1, 128, NC_ * 128], BF16, kind="Internal")
    y1Tscr = nc.dram_tensor("y1Tscr", [QT, 128, NC_, 4, 128], BF16, kind="Internal")
    qTscr = nc.dram_tensor("qTscr", [H, 128, NT * 128], BF16, kind="Internal")
    w2scr = nc.dram_tensor("w2scr", [NF1, 128, NF1 * 128], BF16, kind="Internal")
    w3scr = nc.dram_tensor("w3scr", [4, 128, NF1, 512], BF16, kind="Internal")

    TOP = nc.sbuf_top
    S1 = TOP - 64 * 1024
    S2 = TOP - 128 * 1024
    S3 = S2 - 32 * 1024
    S4 = S3 - 40 * 1024

    def at(name, shape, dtype, off):
        return nc.alloc_sbuf_tensor_at(name, list(shape), dtype, offset=off)

    def sm(name, shape, dtype=F32):
        return nc.alloc_sbuf_tensor(name, list(shape), dtype)

    ident_f = sm("ident_f", [128, 128], F32)
    ident_b = sm("ident_b", [128, 128], BF16)
    jrow = sm("jrow", [128, 128], F32)
    ones_b = sm("ones_b", [1, 128], BF16)
    sw1_b = sm("sw1_b", [128, 128], BF16)
    sw2_b = sm("sw2_b", [128, 1], BF16)
    sb1_s = sm("sb1_s", [128, 1], F32)
    sb2_s = sm("sb2_s", [1, H], BF16)
    b1_s = sm("b1_s", [128, NF1], F32)
    b2_s = sm("b2_s", [128, NF1], F32)
    zcol = sm("zcol", [128, 1], F32)
    eps_c = sm("eps_c", [128, 1], F32)
    m_c = sm("m_c", [128, 1], F32)
    nm_c = sm("nm_c", [128, 1], F32)
    se_c = sm("se_c", [128, 1], F32)
    rc_c = sm("rc_c", [128, 1], F32)
    mean_c = sm("mean_c", [128, 2], F32)
    sq_c = sm("sq_c", [128, 2], F32)
    var_c = sm("var_c", [128, 2], F32)
    sd_c = sm("sd_c", [128, 2], F32)
    rs_c = sm("rs_c", [128, 2], F32)
    lo_c = sm("lo_c", [H, 1], F32)
    hi_c = sm("hi_c", [H, 1], F32)
    mid_c = sm("mid_c", [H, 1], F32)
    mids_c = sm("mids_c", [H, 1], F32)
    cnt_c = sm("cnt_c", [H, 1], F32)
    sel_c = sm("sel_c", [H, 1], F32)
    d0_c = sm("d0_c", [H, 1], F32)
    mx4 = sm("mx4", [128, 2, 4], F32)
    nm4 = sm("nm4", [128, 2, 4], F32)
    se4 = sm("se4", [128, 2, 4], F32)
    rc4 = sm("rc4", [128, 2, 4], F32)
    kvt_sb = sm("kvt_sb", [128, NC_ * H], F32)
    rpt = sm("rpt", [128, NC_, H], F32)


    xT_bf = at("xT_bf", [128, NC_, S], BF16, S2)
    woT_bf = at("woT_bf", [128, NC_, DM], BF16, S2)
    k_tok = at("k_tok", [128, NT, DM], BF16, S1)
    v_tok = at("v_tok", [128, NT, DM], BF16, S1)
    aT = at("aT", [128, H, S], BF16, S1)

    kvA = at("kvA", [H, S], F32, S3)
    kvB = at("kvB", [H, S], F32, S3 + 8 * 1024)
    kvC = at("kvC", [H, S], F32, S3 + 16 * 1024)
    kselT = at("kselT", [128, H, 128], BF16, S3 + 24 * 1024)
    vsel = at("vsel", [128, H, 128], BF16, S3 + 28 * 1024)
    qT_hb = at("qT_hb", [128, NT, 128], BF16, S4 + 8 * 1024)
    attn_b = at("attn_b", [128, 2, 4, 128], BF16, S4 + 12 * 1024)
    attnn_b = at("attnn_b", [128, 2, 4, 128], BF16, S4 + 14 * 1024)
    attnt_b = at("attnt_b", [128, 2, 4, 128], BF16, S4 + 16 * 1024)
    xtok_t = at("xtok_t", [128, 2, DM], F32, S3)
    u1_t = at("u1_t", [128, DM], F32, S3 + 16 * 1024)
    y1a_t = at("y1a_t", [128, DM], F32, S3 + 24 * 1024)
    y1b_t = at("y1b_t", [128, DM], F32, S4)
    y1q = at("y1q", [128, 4, DM], F32, S3)
    y1b16 = [at("y1b16a", [128, DM], BF16, S4 + 16 * 1024),
             at("y1b16b", [128, DM], BF16, S4 + 20 * 1024)]
    u1_t2 = at("u1b_t", [128, DM], F32, S4 + 24 * 1024)
    pstg = [at("pstg0", [128, 2048], BF16, S4 + 32 * 1024),
            at("pstg1", [128, 2048], BF16, S4 + 36 * 1024)]
    sqscr = at("sqscr", [128, DM], F32, S4 + 28 * 1024)

    wslab = [at("wslab0", [128, NC_, 512], BF16, S4),
             at("wslab1", [128, NC_, 512], BF16, S4 + 16 * 1024)]
    kT_h = at("kT_h", [128, NT, 128], BF16, S3)
    h1sT = at("h1sT", [128, NT, 128], BF16, S3 + 4 * 1024)
    mt_b = [at("mt_b0", [128, NC_, 128], BF16, S3 + 8 * 1024),
            at("mt_b1", [128, NC_, 128], BF16, S3 + 12 * 1024)]
    qld = [at("qld0", [128, NT, 128], BF16, S3 + 8 * 1024),
           at("qld1", [128, NT, 128], BF16, S3 + 12 * 1024)]
    ksel_tmp = at("ksel_tmp", [128, 128], BF16, S3 + 16 * 1024)
    wqsl = [at("wqsl0", [128, NC_, 128], BF16, S4),
            at("wqsl1", [128, NC_, 128], BF16, S4 + 4 * 1024)]
    w1sl = [at("w1sl0", [128, NC_, 128], BF16, S4),
            at("w1sl1", [128, NC_, 128], BF16, S4 + 4 * 1024)]
    w2sl = [at("w2sl0", [128, NF1, 128], BF16, S4 + 8 * 1024),
            at("w2sl1", [128, NF1, 128], BF16, S4 + 16 * 1024)]
    b3_s = at("b3_s", [1, DM], BF16, S4 + 24 * 1024)

    y1T_q = [at("y1T_qa", [128, NC_, QS], BF16, S2),
             at("y1T_qb", [128, NC_, QS], BF16, S2 + 16 * 1024)]
    h1T_q = at("h1T_q", [128, NF1, QS], BF16, S2 + 32 * 1024)
    h2T_q = at("h2T_q", [128, NF1, QS], BF16, S1)
    w3sl = [at("w3sl0", [128, NF1, 512], BF16, S1 + 32 * 1024),
            at("w3sl1", [128, NF1, 512], BF16, S2 + 32 * 1024)]


    pf = [nc.alloc_psum_tensor("pf%d" % i, [128, 512], F32) for i in range(4)]
    pb = [nc.alloc_psum_tensor("pb%d" % i, [128, 512], BF16) for i in range(3)]
    pkv = nc.alloc_psum_tensor("pkv", [128, 512], F32)

    nc.const_aps.register(F32, 0.0, zcol[:, :])
    p = Plan(nc)
    ps_free = {t.name: None for t in pf + pb + [pkv]}
    ps_ready = {}

    def mmgroup(psum_name, ops, deps=()):
        d = list(deps)
        if ps_free[psum_name] is not None:
            d.append(ps_free[psum_name])
        for fn in ops[:-1]:
            p.emit("T", fn, deps=tuple(d))
            d = ()
        tokn = p.chain("T", ops[-1], deps=tuple(d))
        ps_ready[psum_name] = tokn
        return tokn

    def evac(eng, fn, psum_name, deps=()):
        d = list(deps) + [ps_ready[psum_name]]
        tokn = p.chain(eng, fn, deps=d)
        ps_free[psum_name] = tokn
        return tokn

    def transp(pn_ap, pn_name, fn, deps=()):
        d = list(deps)
        if ps_free[pn_name] is not None:
            d.append(ps_free[pn_name])
        tokn = p.chain("T", fn, deps=d)
        ps_ready[pn_name] = tokn
        return tokn

    def gdma(dst, src, deps=(), sname="GD"):
        return p.emit("G", (lambda g, dst=dst, src=src: g.dma_start(dst, src)),
                      deps=deps, inc=(sname, 16))

    def qdma(dst, src, deps=(), sname="QD"):
        return p.emit("Q", (lambda q, dst=dst, src=src: q.dma_start(dst, src)),
                      deps=deps, inc=(sname, 16))

    def vo(fn, deps=()):
        return p.chain("V", fn, deps=deps)

    def so(fn, deps=()):
        return p.chain("S", fn, deps=deps)

    def gop(fn, deps=()):
        return p.chain("G", fn, deps=deps)

    for i in range(4):
        gdma(xT_bf[:, 4 * i:4 * i + 4, :],
             xt_h[4 * i:4 * i + 4, :, :].transpose([1, 0, 2]), sname="XT")
    XT_TOK = ("XT", p.cnt["XT"])
    gdma(sw1_b[:, :], sw1T_h[:, :], sname="SML")
    gdma(sw2_b[:, :], sw2_h[:, :], sname="SML")
    gdma(sb1_s[:, :], sb1_h[:, :], sname="SML")
    gdma(sb2_s[:, :], sb2_h[:, :], sname="SML")
    gdma(b1_s[:, :], b1_h[:, :], sname="SML")
    gdma(b2_s[:, :], b2_h[:, :], sname="SML")
    SML_TOK = ("SML", p.cnt["SML"])

    gop(lambda g: g.memset(ident_f[:, :], 0.0))
    gop(lambda g: g.memset(ident_b[:, :], 0.0))
    gop(lambda g: g.memset(zcol[:, :], 0.0))
    gop(lambda g: g.memset(eps_c[:, :], EPS))
    gop(lambda g: g.memset(ones_b[:, :], 1.0))
    gop(lambda g: g.affine_select(out=ident_f[:, :], in_=ident_f[:, :],
                                  compare_op=ALU.not_equal, fill=1.0, base=0,
                                  pattern=[[-1, 128]], channel_multiplier=1))
    gop(lambda g: g.affine_select(out=ident_b[:, :], in_=ident_b[:, :],
                                  compare_op=ALU.not_equal, fill=1.0, base=0,
                                  pattern=[[-1, 128]], channel_multiplier=1))
    SETUP_TOK = gop(lambda g: g.iota(jrow[:, :], pattern=[[1, 128]], base=1,
                                     channel_multiplier=0,
                                     allow_small_or_imprecise_dtypes=True))

    prep_units = []
    for f1 in range(NF1):
        prep_units.append(("w1", f1, 0))
    for f2 in range(NF1):
        prep_units.append(("w2", f2, 0))
        prep_units.append(("w2", f2, 1))
    for f3 in range(4):
        for i in range(8):
            prep_units.append(("w3", f3, i))
    prep_state = {"i": 0, "war": [None, None]}

    def prep_pump(n):
        for _ in range(n):
            i = prep_state["i"]
            if i >= len(prep_units):
                return
            prep_state["i"] = i + 1
            kind, a, b = prep_units[i]
            buf = i % 2
            st = pstg[buf]
            d = []
            if prep_state["war"][buf] is not None:
                d.append(prep_state["war"][buf])
            if kind == "w1":
                gdma(st[:, :], w1L_h[a].rearrange("p c m -> p (c m)"),
                     deps=tuple(d), sname="PG%d" % buf)
            elif kind == "w2":
                gdma(st[:, :],
                     w2L_h[a, :, 16 * b:16 * b + 16, :].rearrange("p c m -> p (c m)"),
                     deps=tuple(d), sname="PG%d" % buf)
            else:
                gdma(st[:, :].rearrange("p (c x) -> p c x", c=4),
                     w3T_h[4 * b:4 * b + 4, :, 512 * a:512 * (a + 1)]
                     .transpose([1, 0, 2]),
                     deps=tuple(d), sname="PG%d" % buf)
            gtok = ("PG%d" % buf, p.cnt["PG%d" % buf])
            if kind == "w1":
                qdma(w1scr[a], st[:, :], deps=[gtok], sname="PQ%d" % buf)
            elif kind == "w2":
                qdma(w2scr[a, :, 2048 * b:2048 * (b + 1)], st[:, :], deps=[gtok],
                     sname="PQ%d" % buf)
            else:
                qdma(w3scr[a, :, 4 * b:4 * b + 4, :],
                     st[:, :].rearrange("p (c x) -> p c x", c=4), deps=[gtok],
                     sname="PQ%d" % buf)
            prep_state["war"][buf] = ("PQ%d" % buf, p.cnt["PQ%d" % buf])

    slab_war = [None, None]
    slab_tok = [None, None]

    def proj_token_major(w_dram, dest, base_deps, s_only=False, after_fb=None):
        for fb in range(4):
            buf = fb % 2
            d = [SETUP_TOK] + list(base_deps)
            if slab_war[buf] is not None:
                d.append(slab_war[buf])
            for piece in range(2):
                gdma(wslab[buf][:, 8 * piece:8 * piece + 8, :],
                     w_dram[8 * piece:8 * piece + 8, :, 512 * fb:512 * (fb + 1)]
                     .transpose([1, 0, 2]),
                     deps=tuple(d), sname="WS%d" % buf)
                d = []
            slab_tok[buf] = ("WS%d" % buf, p.cnt["WS%d" % buf])
            for ts in range(NT):
                pn = pf[ts % 2]
                ops = []
                for dc in range(NC_):
                    def mk(t, dc=dc, ts=ts, pn=pn, buf=buf):
                        return t.matmul(pn[:, :],
                                        xT_bf[:, dc, 128 * ts:128 * (ts + 1)],
                                        wslab[buf][:, dc, :],
                                        start=(dc == 0), stop=(dc == NC_ - 1))
                    ops.append(mk)
                mmgroup(pn.name, ops, deps=[XT_TOK, slab_tok[buf]])
                eng = "S" if s_only else ("V" if ts % 2 == 0 else "S")
                if eng == "V":
                    def ev(e, ts=ts, fb=fb, pn=pn):
                        return e.tensor_copy(dest[:, ts, 512 * fb:512 * (fb + 1)],
                                             pn[:, :])
                else:
                    def ev(e, ts=ts, fb=fb, pn=pn):
                        return e.copy(dest[:, ts, 512 * fb:512 * (fb + 1)], pn[:, :])
                evac(eng, ev, pn.name)
            slab_war[buf] = p.tok("T")
            prep_pump(8)
            if after_fb is not None:
                after_fb(fb, p.tok("V"), p.tok("S"))

    kv_state = {"first": True}

    def kv_head_pipeline(h, vtok, stok):
        for ts in range(NT):
            pbn = pb[ts % 2]
            def tr(t, ts=ts, h=h, pbn=pbn):
                return t.transpose(pbn[:, 0:128],
                                   k_tok[:, ts, 128 * h:128 * (h + 1)],
                                   ident_b[:, :])
            transp(pbn, pbn.name, tr, deps=[SETUP_TOK, vtok, stok])
            evac("V", (lambda e, ts=ts, pbn=pbn:
                       e.tensor_copy(kT_h[:, ts, :], pbn[:, 0:128])), pbn.name)
        kT_tok = p.tok("V")
        for tb in range(4):
            pn = pf[2 + tb % 2]
            def mm(t, tb=tb, pn=pn):
                return t.matmul(pn[:, :], sw1_b[:, :], kT_h[:, 4 * tb:4 * tb + 4, :],
                                start=True, stop=True)
            mmgroup(pn.name, [mm], deps=[kT_tok, SML_TOK])
            def ev(e, tb=tb, pn=pn):
                return e.activation(h1sT[:, 4 * tb:4 * tb + 4, :], pn[:, :],
                                    AF.Relu, bias=sb1_s[:, :], scale=1.0)
            evac("S", ev, pn.name)
        h1_tok = p.tok("S")
        prep_pump(3)
        d = [h1_tok, SML_TOK]
        for c in range(NC_):
            first = kv_state["first"]
            kv_state["first"] = False
            def mk(t, c=c, h=h, first=first):
                return t.matmul(pkv[:, c * H + h:c * H + h + 1],
                                h1sT[:, c, :], sw2_b[:, :],
                                start=first, stop=False)
            p.emit("T", mk, deps=tuple(d))
            d = []

    def p1_after_fb(fb, vtok, stok):
        for h in range(4 * fb, 4 * fb + 4):
            kv_head_pipeline(h, vtok, stok)

    proj_token_major(wkT_h, k_tok, [], after_fb=p1_after_fb)
    for c in range(NC_):
        def mk(t, c=c):
            return t.matmul(pkv[:, H * c:H * (c + 1)], ones_b[:, :], sb2_s[0:1, :],
                            start=False, stop=(c == NC_ - 1))
        if c == NC_ - 1:
            tokn = p.chain("T", mk, deps=[SML_TOK])
            ps_ready[pkv.name] = tokn
        else:
            p.emit("T", mk, deps=[SML_TOK])
    evac("V", lambda e: e.tensor_copy(kvt_sb[:, :], pkv[:, 0:NC_ * H]), pkv.name)
    kvt_tok = p.tok("V")
    for c in range(NC_):
        pn = pf[c % 2]
        def tr(t, c=c, pn=pn):
            return t.transpose(pn[0:16, 0:128], kvt_sb[:, H * c:H * (c + 1)],
                               ident_f[:, :])
        transp(pn, pn.name, tr, deps=[kvt_tok])
        evac("V", (lambda e, c=c, pn=pn:
                   e.tensor_copy(kvA[:, 128 * c:128 * (c + 1)], pn[0:16, 0:128])),
             pn.name)

    wq_war = [None, None]
    qstg_war = None
    for h in range(H):
        buf = h % 2
        d = [SETUP_TOK, slab_war[0], slab_war[1]]
        if wq_war[buf] is not None:
            d.append(wq_war[buf])
        gdma(wqsl[buf][:, :, :], wqL_h[h], deps=tuple(d), sname="WQ%d" % buf)
        wq_tok = ("WQ%d" % buf, p.cnt["WQ%d" % buf])
        for tb in range(4):
            pn = pf[tb % 2]
            ops = []
            for dc in range(NC_):
                def mk(t, dc=dc, tb=tb, pn=pn, buf=buf):
                    return t.matmul(pn[:, :], wqsl[buf][:, dc, :],
                                    xT_bf[:, dc, 512 * tb:512 * (tb + 1)],
                                    start=(dc == 0), stop=(dc == NC_ - 1))
                ops.append(mk)
            mmgroup(pn.name, ops, deps=[XT_TOK, wq_tok])
            dd = [qstg_war] if (qstg_war is not None and tb == 0) else []
            def ev(e, tb=tb, pn=pn):
                return e.activation(qT_hb[:, 4 * tb:4 * tb + 4, :], pn[:, :],
                                    AF.Copy, bias=0.0, scale=SCALE)
            evac("S", ev, pn.name, deps=dd + [SML_TOK])
        wq_war[buf] = p.tok("T")
        q_stok = p.tok("S")
        qdma(qTscr[h], qT_hb[:, :, :].rearrange("p c m -> p (c m)"),
             deps=[q_stok], sname="QS")
        qstg_war = ("QS", p.cnt["QS"])
    QS_ALL = ("QS", p.cnt["QS"])

    vo(lambda v: v.tensor_reduce(lo_c[:, :], kvA[:, :], axis=AX.X, op=ALU.min))
    vo(lambda v: v.tensor_reduce(hi_c[:, :], kvA[:, :], axis=AX.X, op=ALU.max))
    vo(lambda v: v.tensor_scalar_add(lo_c[:, :], lo_c[:, :], -1.0))
    vo(lambda v: v.tensor_scalar_add(hi_c[:, :], hi_c[:, :], 1.0))
    for it in range(NBIN):
        vo(lambda v: v.tensor_add(mid_c[:, :], lo_c[:, :], hi_c[:, :]))
        vo(lambda v: v.tensor_scalar_mul(mids_c[:, :], mid_c[:, :], 0.5))
        vo(lambda v: v.tensor_scalar(kvB[:, :], kvA[:, :], mids_c[:, :], None,
                                     op0=ALU.is_ge, op1=ALU.add,
                                     accum_out=cnt_c[:, :]))
        vo(lambda v: v.tensor_scalar(sel_c[:, :], cnt_c[:, :], float(KSEL), None,
                                     op0=ALU.is_ge))
        vo(lambda v: v.tensor_sub(d0_c[:, :], mids_c[:, :], lo_c[:, :]))
        vo(lambda v: v.scalar_tensor_tensor(lo_c[:, :], d0_c[:, :], sel_c[:, :],
                                            lo_c[:, :], op0=ALU.mult,
                                            op1=ALU.add))
        vo(lambda v: v.tensor_sub(d0_c[:, :], hi_c[:, :], mids_c[:, :]))
        vo(lambda v: v.scalar_tensor_tensor(hi_c[:, :], d0_c[:, :], sel_c[:, :],
                                            mids_c[:, :], op0=ALU.mult,
                                            op1=ALU.add))
    vo(lambda v: v.tensor_scalar(kvB[:, :], kvA[:, :], lo_c[:, :], None,
                                 op0=ALU.is_ge))
    vo(lambda v: v.tensor_tensor_scan(kvC[:, :], kvB[:, :], kvB[:, :], 0.0,
                                      op0=ALU.add, op1=ALU.bypass))
    RP_TOK = vo(lambda v: v.tensor_mul(kvA[:, :], kvC[:, :], kvB[:, :]))
    for c in range(NC_):
        pn = pf[2 + c % 2]
        def tr(t, c=c, pn=pn):
            return t.transpose(pn[0:128, 0:16], kvA[:, 128 * c:128 * (c + 1)],
                               ident_f[0:16, 0:16])
        transp(pn, pn.name, tr, deps=[RP_TOK])
        evac("V", (lambda e, c=c, pn=pn:
                   e.tensor_copy(rpt[:, c, :], pn[0:128, 0:16])), pn.name)
    RPT_TOK = p.tok("V")

    gather_war = [None, None]

    def gather_mt(h, extra_dep):
        buf = h % 2
        d = [RPT_TOK, extra_dep]
        if gather_war[buf] is not None:
            d.append(gather_war[buf])
        vo((lambda v, h=h, buf=buf: v.tensor_tensor(
            mt_b[buf][:, :, :],
            jrow[:, :].unsqueeze(1).to_broadcast([128, NC_, 128]),
            rpt[:, :, h:h + 1].to_broadcast([128, NC_, 128]),
            op=ALU.is_equal)), deps=d)
        return p.tok("V")

    def gather_mm(dest_sel, src_tok, h, mt_tok, src_deps, transpose_out=False):
        buf = h % 2
        pn = pf[3]
        ops = []
        for c in range(NC_):
            def mk(t, c=c, h=h, pn=pn, buf=buf):
                return t.matmul(pn[:, 0:128], mt_b[buf][:, c, :],
                                src_tok[:, c, 128 * h:128 * (h + 1)],
                                start=(c == 0), stop=(c == NC_ - 1))
            ops.append(mk)
        mmgroup(pn.name, ops, deps=[mt_tok] + list(src_deps))
        gather_war[buf] = p.tok("T")
        if not transpose_out:
            evac("V", (lambda e, h=h: e.tensor_copy(dest_sel[:, h, :],
                                                    pn[:, 0:128])), pn.name)
        else:
            ev_tok = evac("V", (lambda e: e.tensor_copy(ksel_tmp[:, :],
                                                        pn[:, 0:128])), pn.name)
            pbn = pb[0]
            def tr(t, pbn=pbn):
                return t.transpose(pbn[:, 0:128], ksel_tmp[:, :], ident_b[:, :])
            transp(pbn, pbn.name, tr, deps=[ev_tok])
            evac("V", (lambda e, h=h: e.tensor_copy(dest_sel[:, h, :],
                                                    pbn[:, 0:128])), pbn.name)

    def gather_batch(dest_sel, src_tok, heads, extra_dep, src_deps,
                     transpose_out=False):
        mt_toks = {}
        pending = []
        for h in heads:
            mt_toks[h] = gather_mt(h, extra_dep)
            pending.append(h)
            if len(pending) >= 2:
                h0 = pending.pop(0)
                gather_mm(dest_sel, src_tok, h0, mt_toks[h0], src_deps,
                          transpose_out)
        for h0 in pending:
            gather_mm(dest_sel, src_tok, h0, mt_toks[h0], src_deps, transpose_out)


    KG_V = p.tok("V")
    KG_S = p.tok("S")
    gather_batch(kselT, k_tok, range(H), SETUP_TOK, [KG_V, KG_S],
                 transpose_out=True)
    KSEL_DONE = p.tok("V")

    def p3_after_fb(fb, vtok, stok):
        gather_batch(vsel, v_tok, range(4 * fb, 4 * fb + 4), SETUP_TOK,
                     [vtok, stok])

    proj_token_major(wvT_h, v_tok, [p.tok("T"), QS_ALL], s_only=True,
                     after_fb=p3_after_fb)
    VSEL_DONE = p.tok("V")

    P5_T0 = p.tok("T")
    qld_war = [None, None]
    wo_pieces = [0]
    for h in range(H):
        buf = h % 2
        d = [QS_ALL]
        if qld_war[buf] is not None:
            d.append(qld_war[buf])
        else:
            d.append(P5_T0)
        gdma(qld[buf][:, :, :].rearrange("p c m -> p (c m)"), qTscr[h],
             deps=tuple(d), sname="QL%d" % buf)
        q_tok = ("QL%d" % buf, p.cnt["QL%d" % buf])
        qT_h = qld[buf]
        prep_pump(2)
        def attn_scores(qsb):
            pn = pf[2 + qsb % 2]
            ops = []
            for j in range(4):
                def mk(t, j=j, qsb=qsb, h=h, pn=pn, qT_h=qT_h):
                    return t.matmul(pn[:, 128 * j:128 * (j + 1)],
                                    qT_h[:, 4 * qsb + j, :], kselT[:, h, :],
                                    start=(j == 0), stop=(j == 3))
                ops.append(mk)
            mmgroup(pn.name, ops, deps=[q_tok, KSEL_DONE])
            return pn

        def attn_maxnm(qsb, pn):
            pi = qsb % 2
            return vo((lambda v, pn=pn, pi=pi: v.reduce_max(
                nm4[:, pi, :], pn[:, :].rearrange("p (j x) -> p j x", j=4),
                axis=AX.X, negate=True)), deps=[ps_ready[pn.name]])

        def attn_exp(qsb, pn, nm_tok):
            pi = qsb % 2
            ex_tok = None
            for j in range(4):
                def ex(e, j=j, pn=pn, pi=pi):
                    return e.activation(attn_b[:, pi, j, :],
                                        pn[:, 128 * j:128 * (j + 1)],
                                        AF.Exp, bias=nm4[:, pi, j:j + 1],
                                        scale=1.0, accum_out=se4[:, pi, j:j + 1])
                ex_tok = so(ex, deps=[nm_tok] if j == 0 else [])
            ps_free[pn.name] = ex_tok
            return ex_tok

        def attn_norm(qsb, ex_tok):
            pi = qsb % 2
            vo((lambda v, pi=pi: v.reciprocal(rc4[:, pi, :], se4[:, pi, :])),
               deps=[ex_tok])
            for j in range(4):
                vo((lambda v, j=j, pi=pi: v.tensor_scalar_mul(
                    attnn_b[:, pi, j, :], attn_b[:, pi, j, :],
                    rc4[:, pi, j:j + 1])))
            return p.tok("V")

        def attn_trcopy(qsb, an_tok, banks):
            pi = qsb % 2
            at_tok = None
            for j in range(4):
                pbn = banks[j % 2]
                def tr(t, j=j, pbn=pbn, pi=pi):
                    return t.transpose(pbn[:, 128 * (j // 2):128 * (j // 2 + 1)],
                                       attnn_b[:, pi, j, :], ident_b[:, :])
                d = [an_tok]
                if ps_free[pbn.name] is not None:
                    d.append(ps_free[pbn.name])
                ttok = p.chain("T", tr, deps=d)
                ps_ready[pbn.name] = ttok
                def cpy(e, j=j, pbn=pbn, pi=pi):
                    return e.copy(attnt_b[:, pi, j, :],
                                  pbn[:, 128 * (j // 2):128 * (j // 2 + 1)])
                at_tok = so(cpy, deps=[ttok])
                ps_free[pbn.name] = at_tok
            return at_tok

        def attn_av(qsb, at_tok):
            pi = qsb % 2
            pn2 = pf[qsb % 2]
            ops2 = []
            for j in range(4):
                def av(t, j=j, h=h, pn2=pn2, pi=pi):
                    return t.matmul(pn2[:, 128 * j:128 * (j + 1)], vsel[:, h, :],
                                    attnt_b[:, pi, j, :], start=(j == 0),
                                    stop=(j == 3))
                ops2.append(av)
            mmgroup(pn2.name, ops2, deps=[at_tok, VSEL_DONE])
            evac("V", (lambda e, h=h, qsb=qsb, pn2=pn2:
                       e.tensor_copy(aT[:, h, 512 * qsb:512 * (qsb + 1)],
                                     pn2[:, :])), pn2.name)

        for pr in range(2):
            A, B = 2 * pr, 2 * pr + 1
            pnA = attn_scores(A)
            pnB = attn_scores(B)
            nmA = attn_maxnm(A, pnA)
            exA = attn_exp(A, pnA, nmA)
            nmB = attn_maxnm(B, pnB)
            rcA = attn_norm(A, exA)
            exB = attn_exp(B, pnB, nmB)
            atA = attn_trcopy(A, rcA, [pb[0], pb[1]])
            rcB = attn_norm(B, exB)
            atB = attn_trcopy(B, rcB, [pb[2], pb[1]])
            attn_av(A, atA)
            attn_av(B, atB)
        qld_war[buf] = p.tok("T")
    AT_DONE = p.tok("V")

    prep_pump(len(prep_units))
    PREP_DONE0 = ("PQ0", p.cnt["PQ0"])
    PREP_DONE1 = ("PQ1", p.cnt["PQ1"])
    T_AFTER_Q = p.tok("T")
    for i in range(4):
        gdma(woT_bf[:, 4 * i:4 * i + 4, :],
             woT_h[4 * i:4 * i + 4, :, :].transpose([1, 0, 2]),
             deps=([P5_T0] if i == 0 else ()), sname="WO")
    WO_TOK = ("WO", p.cnt["WO"])
    xt_war = [None, None]
    y1_war = [None, None]
    stg_war = [[PREP_DONE0, PREP_DONE1], [PREP_DONE0, PREP_DONE1]]
    pend_y1T = []
    for ts in range(NT):
        buf = ts % 2
        u1 = u1_t if buf == 0 else u1_t2
        d = [AT_DONE]
        if xt_war[buf] is not None:
            d.append(xt_war[buf])
        qdma(xtok_t[:, buf, :], xtok_h[128 * ts:128 * (ts + 1), :], deps=tuple(d),
             sname="XK%d" % buf)
        xk_tok = ("XK%d" % buf, p.cnt["XK%d" % buf])
        for fb in range(4):
            pn = pf[fb % 4]
            ops = []
            for hh in range(H):
                def mk(t, hh=hh, ts=ts, fb=fb, pn=pn):
                    return t.matmul(pn[:, :], aT[:, hh, 128 * ts:128 * (ts + 1)],
                                    woT_bf[:, hh, 512 * fb:512 * (fb + 1)],
                                    start=(hh == 0), stop=(hh == H - 1))
                ops.append(mk)
            mmgroup(pn.name, ops, deps=[WO_TOK, AT_DONE])
            def ev(e, fb=fb, pn=pn, buf=buf, u1=u1):
                return e.tensor_tensor(u1[:, 512 * fb:512 * (fb + 1)], pn[:, :],
                                       xtok_t[:, buf, 512 * fb:512 * (fb + 1)],
                                       op=ALU.add)
            tokn = evac("V", ev, pn.name, deps=[xk_tok])
            if fb == 3:
                xt_war[buf] = tokn
        if pend_y1T:
            pend_y1T.pop(0)()
        U1_TOK = p.tok("V")
        y1t = y1a_t if buf == 0 else y1b_t
        ysem = "YW%d" % buf
        d_sq = [U1_TOK, SML_TOK]
        if y1_war[buf] is not None:
            d_sq.append(y1_war[buf])
        vo((lambda v, u1=u1, buf=buf: v.reduce_sum(mean_c[:, buf:buf + 1],
                                                   u1[:, :], axis=AX.X)))
        MN_TOK = vo((lambda v, buf=buf: v.tensor_scalar_mul(
            mean_c[:, buf:buf + 1], mean_c[:, buf:buf + 1], 1.0 / DM)))
        SQ_TOK = so((lambda s, y1t=y1t, u1=u1, buf=buf: s.activation(
            y1t[:, :], u1[:, :], AF.Square, bias=zcol[:, :], scale=1.0,
            accum_out=sq_c[:, buf:buf + 1])), deps=d_sq)
        vo((lambda v, buf=buf: v.tensor_scalar_mul(
            var_c[:, buf:buf + 1], sq_c[:, buf:buf + 1], 1.0 / DM)),
           deps=[SQ_TOK])
        vo((lambda v, buf=buf: v.tensor_mul(sd_c[:, buf:buf + 1],
                                            mean_c[:, buf:buf + 1],
                                            mean_c[:, buf:buf + 1])))
        VV_TOK = vo((lambda v, buf=buf: v.tensor_sub(var_c[:, buf:buf + 1],
                                                     var_c[:, buf:buf + 1],
                                                     sd_c[:, buf:buf + 1])))
        SD_TOK = so((lambda s, buf=buf: s.activation(
            sd_c[:, buf:buf + 1], var_c[:, buf:buf + 1], AF.Sqrt,
            bias=eps_c[:, :], scale=1.0)), deps=[VV_TOK])
        RS_TOK = vo((lambda v, buf=buf: v.reciprocal(rs_c[:, buf:buf + 1],
                                                     sd_c[:, buf:buf + 1])),
                    deps=[SD_TOK])
        Y1_TOK = vo((lambda v, y1t=y1t, u1=u1, buf=buf: v.tensor_scalar(
            y1t[:, :], u1[:, :], mean_c[:, buf:buf + 1], rs_c[:, buf:buf + 1],
            op0=ALU.subtract, op1=ALU.mult)), deps=[RS_TOK, MN_TOK])
        qdma(y1s_h[ts, :, :], y1t[:, :], deps=[Y1_TOK], sname=ysem)
        y1_war[buf] = (ysem, p.cnt[ysem])
        yb = y1b16[buf]
        Y1B_TOK = so((lambda s, y1t=y1t, yb=yb: s.copy(yb[:, :], y1t[:, :])),
                     deps=[Y1_TOK])

        def do_y1T(ts=ts, buf=buf, yb=yb, Y1B_TOK=Y1B_TOK):
            stg = pstg[buf]
            ev_toks = []
            for dc in range(NC_):
                pbn = pb[dc % 3]
                def tr(t, dc=dc, pbn=pbn, yb=yb):
                    return t.transpose(pbn[:, 0:128],
                                       yb[:, 128 * dc:128 * (dc + 1)],
                                       ident_b[:, :])
                transp(pbn, pbn.name, tr, deps=[Y1B_TOK, SETUP_TOK])
                eng = "V" if dc % 2 == 0 else "S"
                dd = list(stg_war[buf]) if dc < 2 else []
                if eng == "V":
                    def ev2(e, dc=dc, pbn=pbn, stg=stg):
                        return e.tensor_copy(stg[:, 128 * dc:128 * (dc + 1)],
                                             pbn[:, 0:128])
                else:
                    def ev2(e, dc=dc, pbn=pbn, stg=stg):
                        return e.copy(stg[:, 128 * dc:128 * (dc + 1)],
                                      pbn[:, 0:128])
                ev_toks.append(evac(eng, ev2, pbn.name, deps=dd))
            qdma(y1Tscr[ts // 4, :, :, ts % 4, :],
                 stg[:, :].rearrange("p (c x) -> p c x", c=NC_),
                 deps=[ev_toks[-1], ev_toks[-2]], sname="YT%d" % buf)
            stg_war[buf] = [("YT%d" % buf, p.cnt["YT%d" % buf])]
        pend_y1T.append(do_y1T)
    for fn_ in pend_y1T:
        fn_()
    pend_y1T = []
    Y1W0 = ("YW0", p.cnt["YW0"])
    Y1W1 = ("YW1", p.cnt["YW1"])
    YT0 = ("YT0", p.cnt["YT0"])
    YT1 = ("YT1", p.cnt["YT1"])

    P7_T0 = p.tok("T")
    gdma(b3_s[:, :], b3_h[:, :], deps=[P7_T0], sname="SMX")
    B3_TOK = ("SMX", p.cnt["SMX"])
    w1_war = [None, None]
    w2_war = [None, None]
    w3_war = [None, None]
    y1tq_war = [None, None]
    ln2_prev = None
    yo_prev = []
    def yq_prefetch(qq):
        bufy = qq % 2
        d = [YT0, YT1]
        if y1tq_war[bufy] is not None:
            d.append(y1tq_war[bufy])
        qdma(y1T_q[bufy][:, :, :], y1Tscr[qq].rearrange("p c s x -> p c (s x)"),
             deps=tuple(d), sname="YQ%d" % bufy)
        return ("YQ%d" % bufy, p.cnt["YQ%d" % bufy])

    yq_next = yq_prefetch(0)
    for qq in range(QT):
        bufy = qq % 2
        yq_tok = yq_next
        for f1 in range(NF1):
            buf = f1 % 2
            if qq > 0 and f1 < 2:
                w1_tok = w1_pre[f1]
            else:
                d = [PREP_DONE0, PREP_DONE1]
                d.append(w1_war[buf] if w1_war[buf] is not None else Y1W0)
                if w1_war[buf] is None:
                    d.append(Y1W1)
                gdma(w1sl[buf][:, :, :].rearrange("p c m -> p (c m)"), w1scr[f1],
                     deps=tuple(d), sname="W1%d" % buf)
                w1_tok = ("W1%d" % buf, p.cnt["W1%d" % buf])
            pn = pf[f1 % 4]
            ops = []
            for dc in range(NC_):
                def mk(t, dc=dc, pn=pn, buf=buf, bufy=bufy):
                    return t.matmul(pn[:, :], w1sl[buf][:, dc, :],
                                    y1T_q[bufy][:, dc, :], start=(dc == 0),
                                    stop=(dc == NC_ - 1))
                ops.append(mk)
            mmgroup(pn.name, ops, deps=[w1_tok, yq_tok])
            w1_war[buf] = p.tok("T")
            def ev(e, f1=f1, pn=pn):
                return e.activation(h1T_q[:, f1, :], pn[:, :], AF.Relu,
                                    bias=b1_s[:, f1:f1 + 1], scale=1.0)
            evac("S", ev, pn.name, deps=[SML_TOK])
        y1tq_war[bufy] = p.tok("T")
        H1_TOK = p.tok("S")
        for f2 in range(NF1):
            buf = f2 % 2
            d = [PREP_DONE0, PREP_DONE1]
            d.append(w2_war[buf] if w2_war[buf] is not None else P7_T0)
            qdma(w2sl[buf][:, :, :].rearrange("p c m -> p (c m)"), w2scr[f2],
                 deps=tuple(d), sname="W2%d" % buf)
            w2_tok = ("W2%d" % buf, p.cnt["W2%d" % buf])
            pn = pf[f2 % 4]
            ops = []
            for fc in range(NF1):
                def mk(t, fc=fc, pn=pn, buf=buf):
                    return t.matmul(pn[:, :], w2sl[buf][:, fc, :],
                                    h1T_q[:, fc, :], start=(fc == 0),
                                    stop=(fc == NF1 - 1))
                ops.append(mk)
            mmgroup(pn.name, ops, deps=[w2_tok, H1_TOK])
            w2_war[buf] = p.tok("T")
            def ev(e, f2=f2, pn=pn):
                return e.activation(h2T_q[:, f2, :], pn[:, :], AF.Relu,
                                    bias=b2_s[:, f2:f2 + 1], scale=1.0)
            evac("S", ev, pn.name, deps=[SML_TOK])
        H2_TOK = p.tok("S")
        if qq + 1 < QT:
            yq_next = yq_prefetch(qq + 1)
            w1_pre = []
            for f1 in range(2):
                d = [PREP_DONE0, PREP_DONE1, w1_war[f1 % 2]]
                gdma(w1sl[f1 % 2][:, :, :].rearrange("p c m -> p (c m)"),
                     w1scr[f1], deps=tuple(d), sname="W1%d" % (f1 % 2))
                w1_pre.append(("W1%d" % (f1 % 2), p.cnt["W1%d" % (f1 % 2)]))
        d = [Y1W0, Y1W1] + yo_prev
        if ln2_prev is not None:
            d.append(ln2_prev)
        qdma(y1q[:, :, :], y1s_h[4 * qq:4 * qq + 4, :, :].transpose([1, 0, 2]),
             deps=tuple(d), sname="Y1R")
        y1r_tok = ("Y1R", p.cnt["Y1R"])
        W2_T = p.tok("T")
        ev_last = [None] * 4
        for f3 in range(4):
            wb = w3sl[f3 % 2]
            d = [PREP_DONE0, PREP_DONE1]
            if f3 % 2 == 1:
                d.append(W2_T)
            d.append(w3_war[f3 % 2] if w3_war[f3 % 2] is not None else T_AFTER_Q)
            gdma(wb[:, :, :], w3scr[f3], deps=tuple(d), sname="W3S%d" % (f3 % 2))
            w3_tok = ("W3S%d" % (f3 % 2), p.cnt["W3S%d" % (f3 % 2)])
            for ts in range(4):
                pn = pf[2 + ts % 2]
                ops = [lambda t, f3=f3, pn=pn: t.matmul(
                    pn[:, :], ones_b[:, :], b3_s[0:1, 512 * f3:512 * (f3 + 1)],
                    start=True, stop=False)]
                for fc in range(NF1):
                    def mk(t, fc=fc, ts=ts, pn=pn, wb=wb):
                        return t.matmul(pn[:, :],
                                        h2T_q[:, fc, 128 * ts:128 * (ts + 1)],
                                        wb[:, fc, :], start=False,
                                        stop=(fc == NF1 - 1))
                    ops.append(mk)
                mmgroup(pn.name, ops, deps=[w3_tok, B3_TOK, H2_TOK])
                def ev(e, ts=ts, f3=f3, pn=pn):
                    return e.tensor_tensor(y1q[:, ts, 512 * f3:512 * (f3 + 1)],
                                           pn[:, :],
                                           y1q[:, ts, 512 * f3:512 * (f3 + 1)],
                                           op=ALU.add)
                ev_last[ts] = evac("V", ev, pn.name, deps=[y1r_tok])
            w3_war[f3 % 2] = p.tok("T")
        for ts in range(4):
            vo((lambda v, ts=ts: v.reduce_sum(mean_c[:, 0:1], y1q[:, ts, :],
                                              axis=AX.X)), deps=[ev_last[ts]])
            MN2 = vo(lambda v: v.tensor_scalar_mul(mean_c[:, 0:1], mean_c[:, 0:1],
                                                   1.0 / DM))
            d_sq = [ev_last[ts], SML_TOK]
            if qq == 0 and ts == 0:
                d_sq += [YT0, YT1]
            SQ2 = so((lambda s, ts=ts: s.activation(
                sqscr[:, :], y1q[:, ts, :], AF.Square, bias=zcol[:, :], scale=1.0,
                accum_out=sq_c[:, 0:1])), deps=d_sq)
            vo(lambda v: v.tensor_scalar_mul(var_c[:, 0:1], sq_c[:, 0:1], 1.0 / DM),
               deps=[SQ2])
            vo(lambda v: v.tensor_mul(sd_c[:, 0:1], mean_c[:, 0:1], mean_c[:, 0:1]),
               deps=[MN2])
            VV2 = vo(lambda v: v.tensor_sub(var_c[:, 0:1], var_c[:, 0:1], sd_c[:, 0:1]))
            SD2 = so(lambda s: s.activation(sd_c[:, 0:1], var_c[:, 0:1], AF.Sqrt,
                                            bias=eps_c[:, :], scale=1.0),
                     deps=[VV2])
            RS2 = vo(lambda v: v.reciprocal(rs_c[:, 0:1], sd_c[:, 0:1]), deps=[SD2])
            Y2 = vo((lambda v, ts=ts: v.tensor_scalar(
                y1q[:, ts, :], y1q[:, ts, :], mean_c[:, 0:1], rs_c[:, 0:1],
                op0=ALU.subtract, op1=ALU.mult)), deps=[RS2])
            qdma(y_h[128 * (4 * qq + ts):128 * (4 * qq + ts + 1), :],
                 y1q[:, ts, :], deps=[Y2], sname="YO%d" % (ts % 2))
        ln2_prev = p.tok("V")
        yo_prev = [("YO0", p.cnt["YO0"]), ("YO1", p.cnt["YO1"])]

    p.emit("Q", lambda q: None, deps=[("YO0", p.cnt["YO0"]),
                                      ("YO1", p.cnt["YO1"])])

    with nc.Block() as block:
        @block.tensor
        def _(t):
            p.replay("T", t)

        @block.vector
        def _(v):
            p.replay("V", v)

        @block.scalar
        def _(s):
            p.replay("S", s)

        @block.sync
        def _(q):
            p.replay("Q", q)

        @block.gpsimd
        def _(g):
            p.replay("G", g)

    return nc


_NC_CACHE = None


def _get_nc():
    global _NC_CACHE
    if _NC_CACHE is None:
        _NC_CACHE = build_nc()
    return _NC_CACHE


def kernel(L0_attn_x, L0_attn_wq, L0_attn_wk, L0_attn_wv, L0_attn_wo,
           L0_attn_sw1, L0_attn_sb1, L0_attn_sw2, L0_attn_sb2,
           L0_ffn_W1, L0_ffn_b1, L0_ffn_W2, L0_ffn_b2, L0_ffn_W3, L0_ffn_b3):
    x = np.asarray(L0_attn_x, dtype=np.float32)
    B = x.shape[0]
    C = np.ascontiguousarray

    wq = np.asarray(L0_attn_wq, np.float32)
    wk = np.asarray(L0_attn_wk, np.float32)
    wv = np.asarray(L0_attn_wv, np.float32)
    wo = np.asarray(L0_attn_wo, np.float32)
    W1 = np.asarray(L0_ffn_W1, np.float32)
    W2 = np.asarray(L0_ffn_W2, np.float32)
    W3 = np.asarray(L0_ffn_W3, np.float32)

    shared = {
        "wkT": C(wk.T.reshape(NC_, 128, DM)),
        "wvT": C(wv.T.reshape(NC_, 128, DM)),
        "woT": C(wo.T.reshape(NC_, 128, DM)),
        "wqL": C(wq.reshape(H, 128, NC_, 128).transpose(0, 3, 2, 1)),
        "sw1T": C(np.asarray(L0_attn_sw1, np.float32).T),
        "sb1c": C(np.asarray(L0_attn_sb1, np.float32).reshape(DH, 1)),
        "sw2c": C(np.asarray(L0_attn_sw2, np.float32).reshape(1, DH).T),
        "sb2r": C(np.broadcast_to(
            np.asarray(L0_attn_sb2, np.float32).reshape(1, 1), (1, H))),
        "w1L": C(W1.reshape(NF1, 128, NC_, 128).transpose(0, 3, 2, 1)),
        "b1b": C(np.asarray(L0_ffn_b1, np.float32).reshape(NF1, 128).T),
        "w2L": C(W2.reshape(NF1, 128, NF1, 128).transpose(0, 3, 2, 1)),
        "b2b": C(np.asarray(L0_ffn_b2, np.float32).reshape(NF1, 128).T),
        "w3T": C(W3.T.reshape(NF1, 128, DM)),
        "b3r": C(np.asarray(L0_ffn_b3, np.float32).reshape(1, DM)),
    }
    in_maps = []
    for b in range(B):
        m = dict(shared)
        m["xt"] = C(x[b].T.reshape(NC_, 128, S))
        m["xtok"] = C(x[b])
        in_maps.append(m)

    nc = _get_nc()
    res = run_bass_kernel_spmd(nc, in_maps, core_ids=list(range(B)))
    return np.stack([r["y"] for r in res.results], axis=0)
